# version 07
# baseline (speedup 1.0000x reference)
import contextlib
import math

import numpy as np
import ml_dtypes

import concourse.bass as bass
import concourse.mybir as mybir
from concourse.bass_utils import run_bass_kernel_spmd

F32 = mybir.dt.float32
BF16 = mybir.dt.bfloat16
I32 = mybir.dt.int32
I16 = mybir.dt.int16
AF = mybir.ActivationFunctionType
ALU = mybir.AluOpType
AX = mybir.AxisListType

NCORE = 8
S = 8192
D = 4096
NH = 32
DH = 128
NG = 32
TOWN = 1024
THALO = 1536
I_FFN = 11008
NIC = 86
NHS = 8
NFS = 16
SCALE = 1.0 / math.sqrt(DH)
EPS = 1e-5
BF16NP = ml_dtypes.bfloat16

ENGS = ("sync", "gpsimd", "tensor", "vector", "scalar")
DEBUG_TAPS = []


class Sem:
    def __init__(self, stack, nc, name):
        self.h = stack.enter_context(nc.semaphore(name))
        self.v = 0


class Plan:

    def __init__(self, fence_bufs=None):
        self.items = {e: [] for e in ENGS}
        self.fence_bufs = fence_bufs or {}

    def op(self, eng, emit, waits=(), incs=(), fence=False):
        waits = [(s.h, val) for (s, val) in waits if val > 0]
        incs2 = []
        for (s, n) in incs:
            s.v += n
            incs2.append((s.h, n))
        self.items[eng].append((waits, emit, incs2,
                                self.fence_bufs.get(eng) if fence else None))

    def run(self, nc):
        def replay(obj, recs, eng):
            for waits, emit, incs, fb in recs:
                for h, val in waits:
                    obj.wait_ge(h, val)
                ins = emit(obj)
                if fb is not None:
                    buf, fcnt = fb
                    c = fcnt[0] % 2048
                    fcnt[0] += 1
                    if eng == "vector":
                        ins = obj.memset(buf[0:1, c:c + 1], 0.0)
                    else:
                        ins = obj.memzero(buf[0:1, c:c + 1])
                for h, n in incs:
                    ins.then_inc(h, n)

        with nc.Block() as blk:
            for e in ENGS:
                recs = self.items[e]
                if recs:
                    getattr(blk, e)(
                        lambda obj, recs=recs, e=e: replay(obj, recs, e))


def build_nc():
    nc = bass.Bass("TRN2", target_bir_lowering=False)

    t_xs = nc.dram_tensor("xs", [12, 128, D], F32, kind="ExternalInput").ap()
    t_wqt = nc.dram_tensor("wqt", [NH, 128, D], F32, kind="ExternalInput").ap()
    t_wkt = nc.dram_tensor("wkt", [128, D], F32, kind="ExternalInput").ap()
    t_wvt = nc.dram_tensor("wvt", [128, D], F32, kind="ExternalInput").ap()
    t_wot = nc.dram_tensor("wot", [NFS, 128, NG * 256], F32, kind="ExternalInput").ap()
    t_gqt = nc.dram_tensor("gqt", [NIC, 128, D], I32, kind="ExternalInput").ap()
    t_uqt = nc.dram_tensor("uqt", [NIC, 128, D], I32, kind="ExternalInput").ap()
    t_dqt = nc.dram_tensor("dqt", [NIC, 128, D], I32, kind="ExternalInput").ap()
    t_gst = nc.dram_tensor("gst", [128, 2752], F32, kind="ExternalInput").ap()
    t_ust = nc.dram_tensor("ust", [128, 2752], F32, kind="ExternalInput").ap()
    t_dst = nc.dram_tensor("dst", [128, 2752], F32, kind="ExternalInput").ap()
    t_cosb = nc.dram_tensor("cosb", [128, THALO], BF16, kind="ExternalInput").ap()
    t_sinb = nc.dram_tensor("sinb", [128, THALO], BF16, kind="ExternalInput").ap()
    t_blend = nc.dram_tensor("blend", [128, 20], F32, kind="ExternalInput").ap()
    t_w4m = nc.dram_tensor("w4m", [128, 512], BF16, kind="ExternalInput").ap()
    t_rmat = nc.dram_tensor("rmat", [128, 128], BF16, kind="ExternalInput").ap()
    t_ident = nc.dram_tensor("ident", [128, 128], BF16, kind="ExternalInput").ap()

    t_y = nc.dram_tensor("y", [8, 128, D], F32, kind="ExternalOutput").ap()

    t_attnb = nc.dram_tensor("attnb", [8, 128, D], F32, kind="Internal").ap()
    t_ytb = nc.dram_tensor("ytb", [NIC, 128, TOWN], BF16, kind="Internal").ap()
    t_gss = nc.dram_tensor("gss", [352256], BF16, kind="Internal").ap()
    t_uss = nc.dram_tensor("uss", [352256], BF16, kind="Internal").ap()
    t_dss = nc.dram_tensor("dss", [352256], BF16, kind="Internal").ap()

    taps = {}
    for name, shape, dt in (("xnt", [128, NG * TOWN], BF16),
                            ("krt", [128, THALO], BF16),
                            ("vsb", [128, 12 * DH], BF16),
                            ("at", [128, NH * TOWN], BF16),
                            ("attnb", [8, 128, D], F32),
                            ("ytb", [NIC, 128, TOWN], BF16)):
        if name in DEBUG_TAPS:
            taps[name] = nc.dram_tensor("tap_" + name, shape, dt,
                                        kind="ExternalOutput").ap()

    st = contextlib.ExitStack()
    with st:
        E = st.enter_context

        sb_cos = E(nc.sbuf_tensor("sb_cos", [128, THALO], BF16))
        sb_sin = E(nc.sbuf_tensor("sb_sin", [128, THALO], BF16))
        sb_blend = E(nc.sbuf_tensor("sb_blend", [128, 20], F32))
        sb_w4m = E(nc.sbuf_tensor("sb_w4m", [128, 512], BF16))
        sb_rmat = E(nc.sbuf_tensor("sb_rmat", [128, 128], BF16))
        sb_id = E(nc.sbuf_tensor("sb_id", [128, 128], BF16))
        xnt = E(nc.sbuf_tensor("xnt", [128, NG * TOWN], BF16))
        krt = E(nc.sbuf_tensor("krt", [128, THALO], BF16))
        v_sb = E(nc.sbuf_tensor("v_sb", [128, 12 * DH], BF16))

        pbank = [E(nc.psum_tensor(f"pb{i}", [128, 512], F32)) for i in range(8)]

        def bankbf(i):
            return pbank[i][:, :].bitcast(BF16)

        fb_v = E(nc.sbuf_tensor("fb_v", [1, 2048], F32))
        fb_s = E(nc.sbuf_tensor("fb_s", [1, 2048], F32))
        FBUFS = {"vector": (fb_v, [0]), "scalar": (fb_s, [0])}

        def Plan_():
            return Plan(FBUFS)

        s_dx = Sem(st, nc, "s_dx")
        s_dg = Sem(st, nc, "s_dg")
        s_tap = Sem(st, nc, "s_tap")
        s_ch = Sem(st, nc, "s_ch")
        chain_n = [0]

        def chain(v, ins):
            chain_n[0] += 1
            ins.then_inc(s_ch.h, 1)
            v.wait_ge(s_ch.h, chain_n[0])
        s_dy = Sem(st, nc, "s_dy")
        s_dw = Sem(st, nc, "s_dw")
        s_db = Sem(st, nc, "s_db")

        with contextlib.ExitStack() as st0:
            E0 = st0.enter_context
            prep = E0(nc.sbuf_tensor("prep", [128, 2752], F32))
            prepo = E0(nc.sbuf_tensor("prepo", [128, 2752], BF16))
            s_pp = Sem(st0, nc, "s_pp")
            pl = Plan_()

            for dst, src in ((sb_cos, t_cosb), (sb_sin, t_sinb),
                             (sb_blend, t_blend), (sb_w4m, t_w4m),
                             (sb_rmat, t_rmat), (sb_id, t_ident)):
                pl.op("sync", lambda sy, d=dst, s=src: sy.dma_start(d[:, :], s),
                      incs=[(s_dx, 16)])

            for k, (tsrc, tdst) in enumerate(((t_gst, t_gss), (t_ust, t_uss),
                                              (t_dst, t_dss))):
                pl.op("gpsimd", lambda sy, s=tsrc: sy.dma_start(prep[:, :], s),
                      waits=[(s_pp, k)], incs=[(s_dg, 16)])
                pl.op("vector",
                      lambda v: v.tensor_copy(prepo[:, :], prep[:, :]),
                      waits=[(s_dg, s_dg.v), (s_dy, 16 * k)], incs=[(s_pp, 1)],
                      fence=True)
                pl.op("sync",
                      lambda sy, d=tdst: sy.dma_start(
                          d.rearrange("(p c) -> p c", p=128), prepo[:, :]),
                      waits=[(s_pp, s_pp.v)], incs=[(s_dy, 16)])
            pl.op("sync", lambda sy: sy.nop(),
                  waits=[(s_dy, s_dy.v), (s_dx, s_dx.v)])
            pl.run(nc)

        with contextlib.ExitStack() as st1:
            E1 = st1.enter_context
            xnth = E1(nc.sbuf_tensor("xnth", [128, NG * 512], BF16))
            xst = [E1(nc.sbuf_tensor(f"xst{i}", [128, D], F32)) for i in range(2)]
            xnb = [E1(nc.sbuf_tensor(f"xnb{i}", [128, D], BF16)) for i in range(2)]
            sqs = E1(nc.sbuf_tensor("sqs", [128, D], BF16))
            mk1 = lambda n: [E1(nc.sbuf_tensor(f"{n}{i}", [128, 1], F32))
                             for i in range(2)]
            st1v, stnm, stss, stvr, stsd, strs = (mk1("st1v"), mk1("stnm"),
                                                  mk1("stss"), mk1("stvr"),
                                                  mk1("stsd"), mk1("strs"))
            wkv = E1(nc.sbuf_tensor("wkv", [128, 2 * NG * 128], BF16))
            ktv = E1(nc.sbuf_tensor("ktv", [128, THALO], BF16))
            rtmp1 = E1(nc.sbuf_tensor("rtmp1", [128, 512], F32))
            rtmp2 = E1(nc.sbuf_tensor("rtmp2", [128, 512], F32))

            s_dga = [Sem(st1, nc, f"s_dga{i}") for i in range(2)]
            s_nm = Sem(st1, nc, "s_nm")
            s_sq = Sem(st1, nc, "s_sq")
            s_var = Sem(st1, nc, "s_var")
            s_sd = Sem(st1, nc, "s_sd")
            s_xn = Sem(st1, nc, "s_xn")
            s_xtp = Sem(st1, nc, "s_xtp")
            s_xv = Sem(st1, nc, "s_xv")

            def xn_dest(ti):
                if ti < 2:
                    return xnth, 512, ti * 128
                if ti < 10:
                    return xnt, TOWN, (ti - 2) * 128
                return xnth, 512, 256 + (ti - 10) * 128

            pl = Plan_()
            pl.op("gpsimd", lambda g: g.dma_start(wkv[:, 0:NG * 128], t_wkt),
                  incs=[(s_dw, 16)])
            pl.op("gpsimd", lambda g: g.dma_start(wkv[:, NG * 128:], t_wvt),
                  incs=[(s_dw, 16)])

            for ti in range(12):
                p = ti % 2
                pl.op("gpsimd",
                      lambda sy, p=p, ti=ti: sy.dma_start(xst[p][:, :], t_xs[ti]),
                      waits=([(s_sq, ti - 1), (s_xn, ti - 1)] if ti >= 2 else []),
                      incs=[(s_dga[p], 16)])

                def a_red(v, p=p):
                    chain(v, v.reduce_sum(st1v[p][:, :], xst[p][:, :], AX.X))
                    return v.tensor_scalar_mul(stnm[p][:, :], st1v[p][:, :],
                                               -1.0 / D)
                pl.op("vector", a_red, waits=[(s_dga[p], 16 * (ti // 2 + 1))],
                      incs=[(s_nm, 1)])

                pl.op("scalar",
                      lambda s, p=p: s.activation(sqs[:, :], xst[p][:, :],
                                                  AF.Square, bias=stnm[p][:, :],
                                                  scale=1.0,
                                                  accum_out=stss[p][:, :]),
                      waits=[(s_nm, s_nm.v), (s_var, ti - 1)], incs=[(s_sq, 1)],
                      fence=True)
                pl.op("vector",
                      lambda v, p=p: v.tensor_scalar(stvr[p][:, :], stss[p][:, :],
                                                     1.0 / D, EPS, ALU.mult,
                                                     ALU.add),
                      waits=[(s_sq, s_sq.v)], incs=[(s_var, 1)])
                pl.op("scalar",
                      lambda s, p=p: s.activation(stsd[p][:, :], stvr[p][:, :],
                                                  AF.Sqrt),
                      waits=[(s_var, s_var.v), (s_xn, ti - 1)], incs=[(s_sd, 1)])

                def a_xn(v, p=p):
                    chain(v, v.reciprocal(strs[p][:, :], stsd[p][:, :]))
                    return v.tensor_scalar(xnb[p][:, :], xst[p][:, :],
                                           stnm[p][:, :], strs[p][:, :],
                                           ALU.add, ALU.mult)
                pl.op("vector", a_xn,
                      waits=[(s_sd, s_sd.v), (s_xtp, 4 * (ti - 1))],
                      incs=[(s_xn, 1)], fence=True)

                dstt, span, coff = xn_dest(ti)
                for jb in range(4):
                    fidx = ti * 4 + jb

                    def a_tp(t, p=p, jb=jb):
                        ins = None
                        for j8 in range(8):
                            j = jb * 8 + j8
                            ins = t.transpose(
                                bankbf(jb % 2)[:, j8 * 128:(j8 + 1) * 128],
                                xnb[p][:, j * 128:(j + 1) * 128], sb_id[:, :])
                        return ins
                    pl.op("tensor", a_tp,
                          waits=[(s_xn, s_xn.v), (s_xv, fidx - 1)],
                          incs=[(s_xtp, 1)])

                    def a_ev(s, jb=jb, dstt=dstt, span=span, coff=coff):
                        dap = dstt[:, :].rearrange("p (g t) -> p g t", t=span)[
                            :, jb * 8:(jb + 1) * 8, coff:coff + 128]
                        sap = bankbf(jb % 2)[:, 0:1024].rearrange(
                            "p (g t) -> p g t", t=128)
                        return s.activation(dap, sap, AF.Copy, bias=0.0,
                                            scale=1.0)
                    pl.op("scalar", a_ev, waits=[(s_xtp, s_xtp.v)],
                          incs=[(s_xv, 1)])
            pl.run(nc)

            s_kpe = Sem(st1, nc, "s_kpe")
            s_kev = Sem(st1, nc, "s_kev")
            s_rot = Sem(st1, nc, "s_rot")
            s_rp = Sem(st1, nc, "s_rp")
            s_vtp = Sem(st1, nc, "s_vtp")
            s_vev = Sem(st1, nc, "s_vev")

            def kv_rhs(c0, grp):
                if c0 < 256:
                    return xnth[:, grp * 512 + c0: grp * 512 + c0 + 256]
                if c0 < 1280:
                    o = c0 - 256
                    return xnt[:, grp * TOWN + o: grp * TOWN + o + 256]
                o = c0 - 1024
                return xnth[:, grp * 512 + o: grp * 512 + o + 256]

            pl = Plan_()
            for which in range(2):
                for b in range(3):
                    def bk_mm(t, which=which, b=b):
                        ins = None
                        for sub in range(2):
                            c0 = b * 512 + sub * 256
                            for j in range(NG):
                                ins = t.matmul(
                                    pbank[b][:, sub * 256:(sub + 1) * 256],
                                    wkv[:, which * NG * 128 + j * 128:
                                        which * NG * 128 + (j + 1) * 128],
                                    kv_rhs(c0, j),
                                    start=(sub == 0 and j == 0),
                                    stop=(sub == 1 and j == NG - 1))
                        return ins
                    waits = [(s_dw, 32)] if (which == 0 and b == 0) else []
                    if which == 1:
                        waits = [(s_kev, b + 1)]
                    pl.op("tensor", bk_mm, waits=waits, incs=[(s_kpe, 1)])

                    ev_waits = [(s_kpe, s_kpe.v)]
                    if which == 1:
                        ev_waits.append((s_rp, b + 1))
                    pl.op("scalar",
                          lambda s, b=b: s.activation(
                              ktv[:, b * 512:(b + 1) * 512], pbank[b][:, :],
                              AF.Copy, bias=0.0, scale=1.0),
                          waits=ev_waits, incs=[(s_kev, 1)], fence=True)

                if which == 0:
                    for b in range(3):
                        pl.op("tensor",
                              lambda t, b=b: t.matmul(
                                  pbank[3 + b][:, :], sb_rmat[:, :],
                                  ktv[:, b * 512:(b + 1) * 512],
                                  start=True, stop=True),
                              waits=[(s_kev, b + 1)], incs=[(s_rot, 1)])

                        def b_rope(v, b=b):
                            sl = slice(b * 512, (b + 1) * 512)
                            v.tensor_mul(rtmp1[:, :], ktv[:, sl], sb_cos[:, sl])
                            chain(v, v.tensor_mul(rtmp2[:, :],
                                                  pbank[3 + b][:, :],
                                                  sb_sin[:, sl]))
                            return v.tensor_add(krt[:, sl], rtmp1[:, :],
                                                rtmp2[:, :])
                        pl.op("vector", b_rope,
                              waits=[(s_rot, s_rot.v), (s_rp, b)],
                              incs=[(s_rp, 1)])

            for fb in range(3):
                def b_vt(t, fb=fb):
                    ins = None
                    for k in range(4):
                        tc = fb * 4 + k
                        ins = t.transpose(
                            bankbf(6 + fb % 2)[:, k * 128:(k + 1) * 128],
                            ktv[:, tc * 128:(tc + 1) * 128], sb_id[:, :])
                    return ins
                pl.op("tensor", b_vt,
                      waits=[(s_kev, 3 + fb + 1), (s_vev, fb - 1)],
                      incs=[(s_vtp, 1)])
                pl.op("scalar",
                      lambda s, fb=fb: s.activation(
                          v_sb[:, fb * 512:(fb + 1) * 512],
                          bankbf(6 + fb % 2)[:, 0:512], AF.Copy, bias=0.0,
                          scale=1.0),
                      waits=[(s_vtp, s_vtp.v)], incs=[(s_vev, 1)])

            if "xnt" in taps:
                pl.op("sync", lambda sy: sy.dma_start(taps["xnt"], xnt[:, :]),
                      waits=[(s_xv, 48)], incs=[(s_dy, 16)])
            if "krt" in taps:
                pl.op("sync", lambda sy: sy.dma_start(taps["krt"], krt[:, :]),
                      waits=[(s_rp, 3)], incs=[(s_dy, 16)])
            if "vsb" in taps:
                pl.op("sync", lambda sy: sy.dma_start(taps["vsb"], v_sb[:, :]),
                      waits=[(s_vev, 3)], incs=[(s_dy, 16)])
            pl.op("sync", lambda sy: sy.nop(), waits=[(s_dy, s_dy.v)])
            pl.run(nc)

        with contextlib.ExitStack() as st2:
            E2 = st2.enter_context
            at_sb = E2(nc.sbuf_tensor("at_sb", [128, NH * TOWN], BF16))

            with contextlib.ExitStack() as st2c:
                E2c = st2c.enter_context
                wq = [E2c(nc.sbuf_tensor(f"wq{i}", [128, D], BF16))
                      for i in range(2)]
                qt_sb = [E2c(nc.sbuf_tensor(f"qt_sb{i}", [128, TOWN], BF16))
                         for i in range(2)]
                qr_sb = [E2c(nc.sbuf_tensor(f"qr_sb{i}", [128, TOWN], BF16))
                         for i in range(2)]
                p_sb = E2c(nc.sbuf_tensor("p_sb", [128, 2048], BF16))
                pt_sb = [E2c(nc.sbuf_tensor(f"pt{i}", [128, 2048], BF16))
                         for i in range(3)]
                qtmp1 = E2c(nc.sbuf_tensor("qtmp1", [128, 512], F32))
                qtmp2 = E2c(nc.sbuf_tensor("qtmp2", [128, 512], F32))
                rs_sb = E2c(nc.sbuf_tensor("rs_sb", [128, 4], F32))
                rc_sb = E2c(nc.sbuf_tensor("rc_sb", [128, 4], F32))
                sv_sb = E2c(nc.sbuf_tensor("sv_sb", [128, 4], F32))

                s_wqd = [Sem(st2c, nc, f"s_wqd{i}") for i in range(2)]
                s_qpe = Sem(st2c, nc, "s_qpe")
                s_qev = Sem(st2c, nc, "s_qev")
                s_rtq = Sem(st2c, nc, "s_rtq")
                s_qr = Sem(st2c, nc, "s_qr")
                s_sc = Sem(st2c, nc, "s_sc")
                s_sg = Sem(st2c, nc, "s_sg")
                s_ps = Sem(st2c, nc, "s_ps")
                s_pt = Sem(st2c, nc, "s_pt")
                s_pv_a = Sem(st2c, nc, "s_pv_a")
                s_pv_v = Sem(st2c, nc, "s_pv_v")
                s_av = Sem(st2c, nc, "s_av")
                s_ate = Sem(st2c, nc, "s_ate")

                W_QOFF = [0, 0, 256, 512, 768]
                W_NQC = [2, 4, 4, 4, 2]
                PT_SLOT = [0, 1, 2, 0, 1]
                AV_CONTRIB = [
                    [(1, 0, 512, 0), (0, 0, 256, 0), (2, 0, 256, 256)],
                    [(3, 0, 512, 0), (2, 256, 256, 0), (4, 0, 256, 256)],
                ]

                pl = Plan_()
                sg_i = 0
                pt_i = 0
                pchunk_free = {c: 0 for c in range(4)}
                slot_free = {0: 0, 1: 0, 2: 0}
                tbank_free = [(s_pv_a, 0), (s_pv_v, 0)]
                sc_after_head = {-1: 0, 0: 0}
                rope_val = {}
                ps_of_chunk = {}
                pv_after_win = {}

                W_QOFF = [0, 0, 256, 512, 768]
                W_NQC = [2, 4, 4, 4, 2]
                PT_SLOT = [0, 1, 2, 0, 1]
                AV_CONTRIB = [
                    [(1, 0, 512, 0), (0, 0, 256, 0), (2, 0, 256, 256)],
                    [(3, 0, 512, 0), (2, 256, 256, 0), (4, 0, 256, 256)],
                ]

                def wq_load(h):
                    par = h % 2
                    pl.op("gpsimd",
                          lambda g, par=par, h=h: g.dma_start(wq[par][:, :],
                                                              t_wqt[h]),
                          waits=[(s_qpe, 2 * (h - 1))],
                          incs=[(s_wqd[par], 16)])

                def q_proj(h, ts):
                    par = h % 2

                    def c_q(t, par=par, ts=ts):
                        ins = None
                        for j in range(NG):
                            ins = t.matmul(
                                pbank[ts][:, :],
                                wq[par][:, j * 128:(j + 1) * 128],
                                xnt[:, j * TOWN + ts * 512:
                                    j * TOWN + ts * 512 + 512],
                                start=(j == 0), stop=(j == NG - 1))
                        return ins
                    pl.op("tensor", c_q,
                          waits=[(s_wqd[par], 16 * (h // 2 + 1)),
                                 (s_qev, 2 * (h - 1) + ts + 1)],
                          incs=[(s_qpe, 1)])
                    pl.op("scalar",
                          lambda s, ts=ts, par=par: s.activation(
                              qt_sb[par][:, ts * 512:(ts + 1) * 512],
                              pbank[ts][:, :], AF.Copy, bias=0.0, scale=1.0),
                          waits=[(s_qpe, s_qpe.v),
                                 (s_qr, rope_val.get(h - 2, 0) if ts == 0
                                  else 0)],
                          incs=[(s_qev, 1)], fence=True)

                def q_rot_rope(h):
                    par = h % 2
                    for ts in range(2):
                        fs, fv = tbank_free[ts]
                        pl.op("tensor",
                              lambda t, ts=ts, par=par: t.matmul(
                                  pbank[4 + ts][:, :], sb_rmat[:, :],
                                  qt_sb[par][:, ts * 512:(ts + 1) * 512],
                                  start=True, stop=True),
                              waits=[(s_qev, 2 * h + 2), (fs, fv)],
                              incs=[(s_rtq, 1)])

                    def c_rope(v, par=par):
                        ins = None
                        for ts in range(2):
                            sl = slice(ts * 512, (ts + 1) * 512)
                            cs = slice(256 + ts * 512, 256 + (ts + 1) * 512)
                            v.tensor_mul(qtmp1[:, :], qt_sb[par][:, sl],
                                         sb_cos[:, cs])
                            chain(v, v.tensor_mul(qtmp2[:, :],
                                                  pbank[4 + ts][:, :],
                                                  sb_sin[:, cs]))
                            ins = v.tensor_add(qr_sb[par][:, sl], qtmp1[:, :],
                                               qtmp2[:, :])
                            if ts == 0:
                                chain(v, ins)
                        return ins
                    pl.op("vector", c_rope,
                          waits=[(s_rtq, s_rtq.v),
                                 (s_sc, sc_after_head.get(h - 2, 0)),
                                 (s_qr, s_qr.v)],
                          incs=[(s_qr, 1)], fence=True)
                    rope_val[h] = s_qr.v
                    tbank_free[0] = (s_qr, s_qr.v)
                    tbank_free[1] = (s_qr, s_qr.v)

                def win_scores(h, w):
                    nonlocal sg_i
                    par = h % 2
                    nqc = W_NQC[w]
                    for c in range(nqc):
                        bki = 2 + (sg_i % 2)
                        pl.op("tensor",
                              lambda t, w=w, c=c, bki=bki, par=par:
                              t.matmul(pbank[bki][:, :],
                                       qr_sb[par][:, W_QOFF[w] + c * 128:
                                                  W_QOFF[w] + c * 128 + 128],
                                       krt[:, 256 * w:256 * w + 512],
                                       start=True, stop=True),
                              waits=[(s_qr, rope_val[h]), (s_sg, sg_i - 1)],
                              incs=[(s_sc, 1)])
                        pl.op("scalar",
                              lambda s, c=c, bki=bki:
                              s.activation(p_sb[:, c * 512:(c + 1) * 512],
                                           pbank[bki][:, :], AF.Sigmoid,
                                           bias=0.0, scale=SCALE,
                                           accum_out=rs_sb[:, c:c + 1]),
                              waits=[(s_sc, s_sc.v), (s_pt, pchunk_free[c])],
                              incs=[(s_sg, 1)], fence=True)

                        def c_pp(v, w=w, c=c):
                            sl = slice(c * 512, (c + 1) * 512)
                            if w == 4:
                                chain(v, v.tensor_mul(
                                    p_sb[:, sl], p_sb[:, sl], sb_w4m[:, :]))
                                chain(v, v.reduce_sum(
                                    rs_sb[:, c:c + 1], p_sb[:, sl], AX.X))
                            chain(v, v.reciprocal(rc_sb[:, c:c + 1],
                                                  rs_sb[:, c:c + 1]))
                            chain(v, v.tensor_mul(
                                sv_sb[:, c:c + 1], rc_sb[:, c:c + 1],
                                sb_blend[:, w * 4 + c:w * 4 + c + 1]))
                            return v.tensor_scalar_mul(
                                p_sb[:, sl], p_sb[:, sl], sv_sb[:, c:c + 1])
                        pl.op("vector", c_pp, waits=[(s_sg, s_sg.v)],
                              incs=[(s_ps, 1)], fence=True)
                        ps_of_chunk[(h, w, c)] = s_ps.v
                        sg_i += 1

                def win_transposes(h, w):
                    nonlocal pt_i
                    nqc = W_NQC[w]
                    slot = PT_SLOT[w]
                    for c in range(nqc):
                        bki = 4 + (pt_i % 2)
                        fs, fv = tbank_free[pt_i % 2]

                        def c_tp(t, c=c, bki=bki):
                            ins = None
                            for kc in range(4):
                                ins = t.transpose(
                                    bankbf(bki)[:, kc * 128:(kc + 1) * 128],
                                    p_sb[:, c * 512 + kc * 128:
                                         c * 512 + kc * 128 + 128],
                                    sb_id[:, :])
                            return ins
                        tw = [(s_ps, ps_of_chunk[(h, w, c)]), (fs, fv)]
                        if c == 0:
                            tw.append((s_av, slot_free[slot]))
                        pl.op("tensor", c_tp, waits=tw, incs=[(s_pt, 1)])
                        pchunk_free[c] = s_pt.v

                        eng = "vector" if pt_i % 2 else "scalar"
                        psem = s_pv_v if pt_i % 2 else s_pv_a

                        def c_te(o, c=c, bki=bki, slot=slot, eng=eng):
                            dap = pt_sb[slot][:, :].rearrange(
                                "p (k q) -> p k q", q=512)[
                                :, :, c * 128:c * 128 + 128]
                            sap = bankbf(bki)[:, 0:512].rearrange(
                                "p (k q) -> p k q", q=128)
                            if eng == "vector":
                                return o.tensor_copy(dap, sap)
                            return o.activation(dap, sap, AF.Copy,
                                                bias=0.0, scale=1.0)
                        pl.op(eng, c_te, waits=[(s_pt, s_pt.v)],
                              incs=[(psem, 1)], fence=True)
                        tbank_free[pt_i % 2] = (psem, psem.v)
                        pt_i += 1
                    pv_after_win[(h, w)] = (s_pv_a.v, s_pv_v.v)

                def av_block(h, bp):
                    needed_a = max(pv_after_win[(h, wv)][0]
                                   for (wv, _, _, _) in AV_CONTRIB[bp])
                    needed_v = max(pv_after_win[(h, wv)][1]
                                   for (wv, _, _, _) in AV_CONTRIB[bp])

                    def c_av(t, bp=bp):
                        first = True
                        ins = None
                        for ci, (w, so, n, do) in enumerate(AV_CONTRIB[bp]):
                            slotw = PT_SLOT[w]
                            for kc in range(4):
                                ins = t.matmul(
                                    pbank[6 + bp][:, do:do + n],
                                    v_sb[:, (2 * w + kc) * 128:
                                         (2 * w + kc + 1) * 128],
                                    pt_sb[slotw][:, kc * 512 + so:
                                                 kc * 512 + so + n],
                                    start=first,
                                    stop=(ci == 2 and kc == 3))
                                first = False
                        return ins
                    pl.op("tensor", c_av,
                          waits=[(s_pv_a, needed_a), (s_pv_v, needed_v),
                                 (s_ate, 2 * (h - 1) + bp + 1)],
                          incs=[(s_av, 1)])
                    for (wv, _, _, _) in AV_CONTRIB[bp]:
                        slot_free[PT_SLOT[wv]] = s_av.v
                    pl.op("vector",
                          lambda v, h=h, bp=bp: v.tensor_copy(
                              at_sb[:, h * TOWN + bp * 512:
                                    h * TOWN + bp * 512 + 512],
                              pbank[6 + bp][:, :]),
                          waits=[(s_av, s_av.v)], incs=[(s_ate, 1)],
                          fence=True)

                wq_load(0)
                q_proj(0, 0)
                q_proj(0, 1)
                q_rot_rope(0)
                for h in range(NH):
                    if h + 1 < NH:
                        wq_load(h + 1)
                    win_scores(h, 0)
                    if h + 1 < NH:
                        q_proj(h + 1, 0)
                    win_transposes(h, 0)
                    win_scores(h, 1)
                    if h + 1 < NH:
                        q_proj(h + 1, 1)
                    win_transposes(h, 1)
                    win_scores(h, 2)
                    win_transposes(h, 2)
                    av_block(h, 0)
                    if h + 1 < NH:
                        q_rot_rope(h + 1)
                    win_scores(h, 3)
                    win_transposes(h, 3)
                    win_scores(h, 4)
                    win_transposes(h, 4)
                    av_block(h, 1)
                    sc_after_head[h] = s_sc.v

                if "at" in taps:
                    pl.op("sync",
                          lambda sy: sy.dma_start(taps["at"], at_sb[:, :]),
                          waits=[(s_ate, s_ate.v)], incs=[(s_dy, 16)])
                    pl.op("sync", lambda sy: sy.nop(), waits=[(s_dy, s_dy.v)])
                pl.run(nc)

            with contextlib.ExitStack() as st2d:
                E2d = st2d.enter_context
                wo = [E2d(nc.sbuf_tensor(f"wo{i}", [128, NG * 256], BF16))
                      for i in range(2)]
                ot_sb = [E2d(nc.sbuf_tensor(f"ot{i}", [128, 256], F32))
                         for i in range(2)]
                s_dwo = [Sem(st2d, nc, f"s_dwo{i}") for i in range(2)]
                s_dpe = Sem(st2d, nc, "s_dpe")
                s_dev = Sem(st2d, nc, "s_dev")
                s_dst2 = [Sem(st2d, nc, f"s_dst2{i}") for i in range(2)]

                pl = Plan_()
                for fs in range(NFS):
                    par = fs % 2
                    pl.op("gpsimd",
                          lambda g, par=par, fs=fs: g.dma_start(wo[par][:, :],
                                                                t_wot[fs]),
                          waits=[(s_dpe, 8 * (fs - 1))],
                          incs=[(s_dwo[par], 16)])
                    for tc in range(8):
                        g = fs * 8 + tc
                        bki = g % 2

                        def d_mm(t, par=par, tc=tc, bki=bki):
                            ins = None
                            for j in range(NG):
                                ins = t.matmul(
                                    pbank[bki][:, 0:256],
                                    at_sb[:, j * TOWN + tc * 128:
                                          j * TOWN + tc * 128 + 128],
                                    wo[par][:, j * 256:(j + 1) * 256],
                                    start=(j == 0), stop=(j == NG - 1))
                            return ins
                        pl.op("tensor", d_mm,
                              waits=[(s_dwo[par], 16 * (fs // 2 + 1)),
                                     (s_dev, g - 1)],
                              incs=[(s_dpe, 1)])
                        pl.op("vector",
                              lambda v, bki=bki, par2=g % 2:
                              v.tensor_copy(ot_sb[par2][:, :],
                                            pbank[bki][:, 0:256]),
                              waits=[(s_dpe, s_dpe.v),
                                     (s_dst2[g % 2], 16 * (g // 2))],
                              incs=[(s_dev, 1)], fence=True)

                        def d_st(sy, tc=tc, fs=fs, par2=g % 2):
                            return sy.dma_start(
                                t_attnb[tc][:, fs * 256:(fs + 1) * 256],
                                ot_sb[par2][:, :])
                        pl.op("sync", d_st, waits=[(s_dev, s_dev.v)],
                              incs=[(s_dst2[g % 2], 16)])
                        if "attnb" in taps:
                            pl.op("sync",
                                  lambda sy, tc=tc, fs=fs, par2=g % 2:
                                  sy.dma_start(
                                      taps["attnb"][tc][:, fs * 256:
                                                        (fs + 1) * 256],
                                      ot_sb[par2][:, :]),
                                  incs=[(s_tap, 16)])
                pl.op("sync", lambda sy: sy.nop(),
                      waits=[(s_dst2[0], s_dst2[0].v), (s_dst2[1], s_dst2[1].v),
                             (s_tap, s_tap.v)])
                pl.run(nc)

        with contextlib.ExitStack() as st3:
            E3 = st3.enter_context
            gq = [E3(nc.sbuf_tensor(f"gq{i}", [128, D], I16)) for i in range(2)]
            uq = [E3(nc.sbuf_tensor(f"uq{i}", [128, D], I16)) for i in range(2)]
            gb = [E3(nc.sbuf_tensor(f"gb{i}", [128, D], BF16)) for i in range(2)]
            ub = [E3(nc.sbuf_tensor(f"ub{i}", [128, D], BF16)) for i in range(2)]
            sg_sb = [E3(nc.sbuf_tensor(f"sg{i}", [128, TOWN], F32))
                     for i in range(2)]
            yt_sb = [E3(nc.sbuf_tensor(f"yt{i}", [128, TOWN], BF16))
                     for i in range(2)]
            s_egq = [Sem(st3, nc, f"s_egq{i}") for i in range(2)]
            s_euq = [Sem(st3, nc, f"s_euq{i}") for i in range(2)]
            s_egb = [Sem(st3, nc, f"s_egb{i}") for i in range(2)]
            s_eub = [Sem(st3, nc, f"s_eub{i}") for i in range(2)]
            s_edq = Sem(st3, nc, "s_edq")
            s_epe = Sem(st3, nc, "s_epe")
            s_esi = Sem(st3, nc, "s_esi")
            s_eyt = Sem(st3, nc, "s_eyt")
            s_dyt = [Sem(st3, nc, f"s_dyt{i}") for i in range(2)]

            pl = Plan_()

            def e_loads(ic):
                par = ic % 2
                pl.op("gpsimd",
                      lambda g, par=par, ic=ic: g.dma_start(gq[par][:, :],
                                                            t_gqt[ic]),
                      waits=[(s_epe, 4 * (ic - 1))], incs=[(s_egq[par], 16)])
                pl.op("gpsimd",
                      lambda g, par=par, ic=ic: g.dma_start(uq[par][:, :],
                                                            t_uqt[ic]),
                      incs=[(s_euq[par], 16)])
                pl.op("gpsimd",
                      lambda g, par=par, ic=ic: g.dma_start(
                          gb[par][:, :],
                          t_gss[ic * D:(ic + 1) * D].partition_broadcast(128)),
                      waits=[(s_edq, 2 * (ic - 1))], incs=[(s_egb[par], 16)])
                pl.op("gpsimd",
                      lambda g, par=par, ic=ic: g.dma_start(
                          ub[par][:, :],
                          t_uss[ic * D:(ic + 1) * D].partition_broadcast(128)),
                      incs=[(s_eub[par], 16)])

            def e_deq(ic):
                par = ic % 2
                nld = 16 * (ic // 2 + 1)
                pl.op("vector",
                      lambda v, par=par: v.scalar_tensor_tensor(
                          gq[par][:, :].bitcast(BF16), gq[par][:, :], -8.0,
                          gb[par][:, :], ALU.add, ALU.mult),
                      waits=[(s_egq[par], nld), (s_egb[par], nld)],
                      incs=[(s_edq, 1)], fence=True)
                pl.op("vector",
                      lambda v, par=par: v.scalar_tensor_tensor(
                          uq[par][:, :].bitcast(BF16), uq[par][:, :], -8.0,
                          ub[par][:, :], ALU.add, ALU.mult),
                      waits=[(s_euq[par], nld), (s_eub[par], nld)],
                      incs=[(s_edq, 1)], fence=True)

            def e_mms(ic):
                par = ic % 2
                for k in range(4):
                    bki = 4 * par + k
                    src_ = gq[par] if k < 2 else uq[par]
                    ts = k % 2
                    wfree = (s_esi,) if k < 2 else (s_eyt,)

                    def e_mm(t, src=src_, ts=ts, bki=bki):
                        ins = None
                        for j in range(NG):
                            ins = t.matmul(
                                pbank[bki][:, :],
                                src[:, j * 128:(j + 1) * 128].bitcast(BF16),
                                xnt[:, j * TOWN + ts * 512:
                                    j * TOWN + ts * 512 + 512],
                                start=(j == 0), stop=(j == NG - 1))
                        return ins
                    pl.op("tensor", e_mm,
                          waits=[(s_edq, 2 * ic + 1 if k < 2 else 2 * ic + 2),
                                 (wfree[0], 2 * (ic - 2) + k % 2 + 1)],
                          incs=[(s_epe, 1)])
                for ts in range(2):
                    pl.op("scalar",
                          lambda s, par=par, ts=ts: s.activation(
                              sg_sb[par][:, ts * 512:(ts + 1) * 512],
                              pbank[4 * par + ts][:, :], AF.Silu, bias=0.0,
                              scale=1.0),
                          waits=[(s_epe, 4 * ic + ts + 1),
                                 (s_eyt, 2 * (ic - 2) + ts + 1)],
                          incs=[(s_esi, 1)], fence=True)

            def e_yt(ic):
                par = ic % 2
                for ts in range(2):
                    pl.op("vector",
                          lambda v, par=par, ts=ts: v.tensor_mul(
                              yt_sb[par][:, ts * 512:(ts + 1) * 512],
                              sg_sb[par][:, ts * 512:(ts + 1) * 512],
                              pbank[4 * par + 2 + ts][:, :]),
                          waits=[(s_esi, 2 * ic + ts + 1),
                                 (s_epe, 4 * ic + 2 + ts + 1),
                                 (s_dyt[ic % 2], 16 * (ic // 2))],
                          incs=[(s_eyt, 1)], fence=True)
                pl.op("sync",
                      lambda sy, par=par, ic=ic: sy.dma_start(t_ytb[ic],
                                                              yt_sb[par][:, :]),
                      waits=[(s_eyt, s_eyt.v)], incs=[(s_dyt[ic % 2], 16)])
                if "ytb" in taps:
                    pl.op("sync",
                          lambda sy, par=par, ic=ic: sy.dma_start(
                              taps["ytb"][ic], yt_sb[par][:, :]),
                          incs=[(s_tap, 16)])

            e_loads(0)
            e_deq(0)
            for ic in range(NIC):
                if ic + 1 < NIC:
                    e_loads(ic + 1)
                    e_deq(ic + 1)
                e_mms(ic)
                e_yt(ic)
            pl.op("sync", lambda sy: sy.nop(),
                  waits=[(s_dyt[0], s_dyt[0].v), (s_dyt[1], s_dyt[1].v),
                         (s_tap, s_tap.v)])
            pl.run(nc)

        with contextlib.ExitStack() as st4:
            E4 = st4.enter_context
            MACROS = [(i * 4, 4) for i in range(21)] + [(84, 2)]
            dqb = [E4(nc.sbuf_tensor(f"dqb{i}", [128, 2048], I16))
                   for i in range(2)]
            db = [E4(nc.sbuf_tensor(f"db{i}", [128, 2048], BF16))
                  for i in range(2)]
            ytl = [E4(nc.sbuf_tensor(f"ytl{i}", [128, 4 * TOWN], BF16))
                   for i in range(2)]
            xev = E4(nc.sbuf_tensor("xev", [128, D], F32))
            aev = E4(nc.sbuf_tensor("aev", [128, D], F32))
            o_sb = [E4(nc.sbuf_tensor(f"osb{i}", [128, 512], F32))
                    for i in range(2)]
            s_fdw = [Sem(st4, nc, f"s_fdw{i}") for i in range(2)]
            s_fdb = [Sem(st4, nc, f"s_fdb{i}") for i in range(2)]
            s_fylp = [Sem(st4, nc, f"s_fylp{i}") for i in range(2)]
            s_dyyp = [Sem(st4, nc, f"s_dyyp{i}") for i in range(2)]
            s_fdq = Sem(st4, nc, "s_fdq")
            s_fpe = Sem(st4, nc, "s_fpe")
            s_fev = Sem(st4, nc, "s_fev")
            s_dyy = Sem(st4, nc, "s_dyy")
            s_fyl = Sem(st4, nc, "s_fyl")

            pl = Plan_()
            for hs in range(NHS):
                pl.op("gpsimd",
                      lambda sy, hs=hs: sy.dma_start(
                          xev[:, :],
                          t_xs[2:10, :, hs * 512:(hs + 1) * 512]
                          .rearrange("a p h -> p a h")),
                      waits=[(s_fev, 8 * hs)], incs=[(s_dg, 16)])
                pl.op("gpsimd",
                      lambda sy, hs=hs: sy.dma_start(
                          aev[:, :],
                          t_attnb[0:8, :, hs * 512:(hs + 1) * 512]
                          .rearrange("a p h -> p a h")),
                      incs=[(s_dg, 16)])

                for mi, (ic0, nm) in enumerate(MACROS):
                    gmac = hs * len(MACROS) + mi
                    par = mi % 2
                    pm = gmac - 2
                    if pm >= 0:
                        pm_hs, pm_mi = divmod(pm, len(MACROS))
                        pm_ic0, pm_nm = MACROS[pm_mi]
                        dq_free = pm_hs * NIC + pm_ic0 + pm_nm
                    else:
                        dq_free = 0
                    pl.op("gpsimd",
                          lambda g, par=par, ic0=ic0, nm=nm, hs=hs:
                          g.dma_start(dqb[par][:, 0:nm * 512],
                                      t_dqt[ic0:ic0 + nm, :,
                                            hs * 512:(hs + 1) * 512]
                                      .rearrange("a p h -> p a h")),
                          waits=[(s_fpe, dq_free)],
                          incs=[(s_fdw[par], 16)])
                    pl.op("sync",
                          lambda sy, par=par, ic0=ic0, nm=nm:
                          sy.dma_start(ytl[par][:, 0:nm * TOWN],
                                       t_ytb[ic0:ic0 + nm]
                                       .rearrange("a p t -> p a t")),
                          waits=[(s_fpe, dq_free)], incs=[(s_fylp[par], 16)])
                    pl.op("gpsimd",
                          lambda s, par=par, ic0=ic0, nm=nm, hs=hs:
                          s.dma_start(
                              db[par][:, 0:nm * 512],
                              t_dss[hs * 44032 + ic0 * 512:
                                    hs * 44032 + (ic0 + nm) * 512]
                              .partition_broadcast(128)),
                          waits=[(s_fdq, gmac - 1)], incs=[(s_fdb[par], 16)])
                    nmac = 16 * (gmac // 2 + 1)
                    pl.op("vector",
                          lambda v, par=par, nm=nm: v.scalar_tensor_tensor(
                              dqb[par][:, 0:nm * 512].bitcast(BF16),
                              dqb[par][:, 0:nm * 512], -8.0,
                              db[par][:, 0:nm * 512], ALU.add, ALU.mult),
                          waits=[(s_fdw[par], nmac), (s_fdb[par], nmac)],
                          incs=[(s_fdq, 1)], fence=True)

                    for ici in range(nm):
                        ic = ic0 + ici

                        def f_mm(t, ic=ic, ici=ici, par=par):
                            ins = None
                            for tc in range(8):
                                ins = t.matmul(
                                    pbank[tc][:, :],
                                    ytl[par][:, ici * TOWN + tc * 128:
                                             ici * TOWN + (tc + 1) * 128],
                                    dqb[par][:, ici * 512:(ici + 1) * 512]
                                    .bitcast(BF16),
                                    start=(ic == 0), stop=(ic == NIC - 1))
                            return ins
                        mmw = [(s_fdq, s_fdq.v),
                               (s_fylp[par], 16 * (gmac // 2 + 1))]
                        if ic == 0:
                            mmw.append((s_fev, 8 * hs))
                        pl.op("tensor", f_mm, waits=mmw, incs=[(s_fpe, 1)])

                for tc in range(8):
                    gev = hs * 8 + tc

                    def f_ev(v, tc=tc, par2=gev % 2):
                        chain(v, v.tensor_add(o_sb[par2][:, :], pbank[tc][:, :],
                                              xev[:, tc * 512:(tc + 1) * 512]))
                        return v.tensor_add(o_sb[par2][:, :], o_sb[par2][:, :],
                                            aev[:, tc * 512:(tc + 1) * 512])
                    pl.op("vector", f_ev,
                          waits=[(s_fpe, (hs + 1) * NIC), (s_dg, s_dg.v),
                                 (s_dyyp[gev % 2], 16 * (gev // 2))],
                          incs=[(s_fev, 1)], fence=True)
                    pl.op("sync",
                          lambda sy, tc=tc, hs=hs, par2=gev % 2:
                          sy.dma_start(t_y[tc][:, hs * 512:(hs + 1) * 512],
                                       o_sb[par2][:, :]),
                          waits=[(s_fev, s_fev.v)], incs=[(s_dyyp[gev % 2], 16)])

            pl.op("sync", lambda sy: sy.nop(),
                  waits=[(s_dyyp[0], s_dyyp[0].v), (s_dyyp[1], s_dyyp[1].v)])
            pl.run(nc)

    return nc


def _rope_tables():
    inv = (1.0 / (10000.0 ** (np.arange(0, DH, 2, dtype=np.float32) / DH))
           ).astype(np.float32)
    tpos = np.arange(S, dtype=np.float32)
    f = np.outer(tpos, inv)
    cosf = np.zeros((S, DH), np.float32)
    sinf = np.zeros((S, DH), np.float32)
    cosf[:, 0::2] = np.cos(f)
    cosf[:, 1::2] = np.cos(f)
    sinf[:, 0::2] = np.sin(f)
    sinf[:, 1::2] = np.sin(f)
    return cosf, sinf


def _blend_for_core(c):
    bl = np.zeros((128, 20), np.float32)
    r = np.arange(128, dtype=np.float32)
    for w in range(5):
        W = 4 * c - 1 + w
        if W < 0:
            continue
        if w == 0:
            for cc in range(2):
                oq = cc * 128 + r
                bl[:, w * 4 + cc] = 1.0 - oq / 255.0
        elif w == 4:
            for cc in range(2):
                wl = cc * 128 + r
                bl[:, w * 4 + cc] = wl / 255.0
        else:
            for cc in range(4):
                wl = cc * 128 + r
                if cc < 2:
                    val = np.ones(128, np.float32) if W == 0 else wl / 255.0
                else:
                    val = 1.0 - (wl - 256.0) / 255.0
                bl[:, w * 4 + cc] = val
    return bl


def kernel(L0_attn_x, L0_attn_Wq, L0_attn_Wk, L0_attn_Wv, L0_attn_Wo,
           L0_ffn_gate_q, L0_ffn_gate_s, L0_ffn_up_q, L0_ffn_up_s,
           L0_ffn_down_q, L0_ffn_down_s):
    x = np.ascontiguousarray(np.asarray(L0_attn_x).reshape(S, D),
                             dtype=np.float32)
    Wq = np.asarray(L0_attn_Wq, np.float32)
    Wk = np.asarray(L0_attn_Wk, np.float32)
    Wv = np.asarray(L0_attn_Wv, np.float32)
    Wo = np.asarray(L0_attn_Wo, np.float32)
    gate_q = np.asarray(L0_ffn_gate_q, np.int32)
    up_q = np.asarray(L0_ffn_up_q, np.int32)
    down_q = np.asarray(L0_ffn_down_q, np.int32)
    gate_s = np.asarray(L0_ffn_gate_s, np.float32)
    up_s = np.asarray(L0_ffn_up_s, np.float32)
    down_s = np.asarray(L0_ffn_down_s, np.float32)

    wqt = np.ascontiguousarray(
        Wq.reshape(NH, DH, NG, 128).transpose(0, 3, 2, 1).reshape(NH, 128, D))
    wkt = np.ascontiguousarray(
        Wk.reshape(DH, NG, 128).transpose(2, 1, 0).reshape(128, D))
    wvt = np.ascontiguousarray(
        Wv.reshape(DH, NG, 128).transpose(2, 1, 0).reshape(128, D))
    wot = np.ascontiguousarray(
        Wo.reshape(NFS, 256, NG, 128).transpose(0, 3, 2, 1)
        .reshape(NFS, 128, NG * 256))
    gqt = np.ascontiguousarray(
        gate_q.reshape(NG, 128, NIC, 128).transpose(2, 1, 0, 3)
        .reshape(NIC, 128, D))
    uqt = np.ascontiguousarray(
        up_q.reshape(NG, 128, NIC, 128).transpose(2, 1, 0, 3)
        .reshape(NIC, 128, D))
    dqt = np.ascontiguousarray(down_q.reshape(NIC, 128, D))
    gst = np.ascontiguousarray(
        gate_s.reshape(NG, NIC, 128).transpose(1, 0, 2).reshape(128, 2752))
    ust = np.ascontiguousarray(
        up_s.reshape(NG, NIC, 128).transpose(1, 0, 2).reshape(128, 2752))
    dst = np.ascontiguousarray(
        down_s.reshape(NIC, NHS, 512).transpose(1, 0, 2).reshape(128, 2752))
    rmat = np.zeros((128, 128), BF16NP)
    for i in range(128):
        rmat[i, (i + 64) % 128] = 1
    ident = np.eye(128).astype(BF16NP)
    cosf, sinf = _rope_tables()

    in_maps = []
    for c in range(NCORE):
        lo = 1024 * c - 256
        xs = np.zeros((THALO, D), np.float32)
        a, b = max(0, lo), min(S, lo + THALO)
        xs[a - lo:b - lo] = x[a:b]
        cpad = np.zeros((THALO, DH), np.float32)
        spad = np.zeros((THALO, DH), np.float32)
        cpad[a - lo:b - lo] = cosf[a:b]
        spad[a - lo:b - lo] = sinf[a:b]
        cosb = np.ascontiguousarray(cpad.T).astype(BF16NP)
        sinT = np.ascontiguousarray(spad.T)
        sinT[0:64, :] *= -1.0
        sinb = sinT.astype(BF16NP)
        w4m = np.ones((128, 512), BF16NP)
        if c == NCORE - 1:
            w4m[:, 256:] = 0
        in_maps.append(dict(
            xs=xs.reshape(12, 128, D), wqt=wqt, wkt=wkt, wvt=wvt, wot=wot,
            gqt=gqt, uqt=uqt, dqt=dqt, gst=gst, ust=ust, dst=dst,
            cosb=cosb, sinb=sinb, blend=_blend_for_core(c), w4m=w4m,
            rmat=rmat, ident=ident))

    nc = build_nc()
    res = run_bass_kernel_spmd(nc, in_maps, core_ids=list(range(NCORE)))
    y = np.concatenate(
        [np.asarray(r["y"], np.float32).reshape(TOWN, D)
         for r in res.results], axis=0)
    if DEBUG_TAPS:
        kernel.last_results = res
    return y.reshape(1, S, D)


# version 09
# speedup vs baseline: 1.0149x; 1.0149x over previous
import contextlib
import math

import numpy as np
import ml_dtypes

import concourse.bass as bass
import concourse.mybir as mybir
from concourse.bass_utils import run_bass_kernel_spmd

F32 = mybir.dt.float32
BF16 = mybir.dt.bfloat16
I32 = mybir.dt.int32
I16 = mybir.dt.int16
AF = mybir.ActivationFunctionType
ALU = mybir.AluOpType
AX = mybir.AxisListType

NCORE = 8
S = 8192
D = 4096
NH = 32
DH = 128
NG = 32
TOWN = 1024
THALO = 1536
I_FFN = 11008
NIC = 86
NHS = 8
NFS = 16
SCALE = 1.0 / math.sqrt(DH)
EPS = 1e-5
BF16NP = ml_dtypes.bfloat16

ENGS = ("sync", "gpsimd", "tensor", "vector", "scalar")
DEBUG_TAPS = []


class Sem:
    def __init__(self, stack, nc, name):
        self.h = stack.enter_context(nc.semaphore(name))
        self.v = 0


class Plan:

    def __init__(self, fence_bufs=None):
        self.items = {e: [] for e in ENGS}
        self.fence_bufs = fence_bufs or {}

    def op(self, eng, emit, waits=(), incs=(), fence=False):
        waits = [(s.h, val) for (s, val) in waits if val > 0]
        incs2 = []
        for (s, n) in incs:
            s.v += n
            incs2.append((s.h, n))
        self.items[eng].append((waits, emit, incs2,
                                self.fence_bufs.get(eng) if fence else None))

    def run(self, nc):
        def replay(obj, recs, eng):
            for waits, emit, incs, fb in recs:
                for h, val in waits:
                    obj.wait_ge(h, val)
                ins = emit(obj)
                if fb is not None:
                    buf, fcnt = fb
                    c = fcnt[0] % 2048
                    fcnt[0] += 1
                    if eng == "vector":
                        ins = obj.memset(buf[0:1, c:c + 1], 0.0)
                    else:
                        ins = obj.memzero(buf[0:1, c:c + 1])
                for h, n in incs:
                    ins.then_inc(h, n)

        with nc.Block() as blk:
            for e in ENGS:
                recs = self.items[e]
                if recs:
                    getattr(blk, e)(
                        lambda obj, recs=recs, e=e: replay(obj, recs, e))


def build_nc():
    nc = bass.Bass("TRN2", target_bir_lowering=False)

    t_xs = nc.dram_tensor("xs", [12, 128, D], F32, kind="ExternalInput").ap()
    t_wqt = nc.dram_tensor("wqt", [NH, 128, D], F32, kind="ExternalInput").ap()
    t_wkt = nc.dram_tensor("wkt", [128, D], F32, kind="ExternalInput").ap()
    t_wvt = nc.dram_tensor("wvt", [128, D], F32, kind="ExternalInput").ap()
    t_wot = nc.dram_tensor("wot", [NFS, 128, NG * 256], F32, kind="ExternalInput").ap()
    t_gqt = nc.dram_tensor("gqt", [NIC, 128, D], I32, kind="ExternalInput").ap()
    t_uqt = nc.dram_tensor("uqt", [NIC, 128, D], I32, kind="ExternalInput").ap()
    t_dqt = nc.dram_tensor("dqt", [NIC, 128, D], I32, kind="ExternalInput").ap()
    t_gst = nc.dram_tensor("gst", [128, 2752], F32, kind="ExternalInput").ap()
    t_ust = nc.dram_tensor("ust", [128, 2752], F32, kind="ExternalInput").ap()
    t_dst = nc.dram_tensor("dst", [128, 2752], F32, kind="ExternalInput").ap()
    t_cosb = nc.dram_tensor("cosb", [128, THALO], BF16, kind="ExternalInput").ap()
    t_sinb = nc.dram_tensor("sinb", [128, THALO], BF16, kind="ExternalInput").ap()
    t_blend = nc.dram_tensor("blend", [128, 20], F32, kind="ExternalInput").ap()
    t_w4m = nc.dram_tensor("w4m", [128, 512], BF16, kind="ExternalInput").ap()
    t_rmat = nc.dram_tensor("rmat", [128, 128], BF16, kind="ExternalInput").ap()
    t_ident = nc.dram_tensor("ident", [128, 128], BF16, kind="ExternalInput").ap()

    t_y = nc.dram_tensor("y", [8, 128, D], F32, kind="ExternalOutput").ap()

    t_attnb = nc.dram_tensor("attnb", [8, 128, D], F32, kind="Internal").ap()
    t_ytb = nc.dram_tensor("ytb", [NIC, 128, TOWN], BF16, kind="Internal").ap()
    t_gss = nc.dram_tensor("gss", [352256], BF16, kind="Internal").ap()
    t_uss = nc.dram_tensor("uss", [352256], BF16, kind="Internal").ap()
    t_dss = nc.dram_tensor("dss", [352256], BF16, kind="Internal").ap()

    taps = {}
    for name, shape, dt in (("xnt", [128, NG * TOWN], BF16),
                            ("krt", [128, THALO], BF16),
                            ("vsb", [128, 12 * DH], BF16),
                            ("at", [128, NH * TOWN], BF16),
                            ("attnb", [8, 128, D], F32),
                            ("ytb", [NIC, 128, TOWN], BF16)):
        if name in DEBUG_TAPS:
            taps[name] = nc.dram_tensor("tap_" + name, shape, dt,
                                        kind="ExternalOutput").ap()

    st = contextlib.ExitStack()
    with st:
        E = st.enter_context

        sb_cos = E(nc.sbuf_tensor("sb_cos", [128, THALO], BF16))
        sb_sin = E(nc.sbuf_tensor("sb_sin", [128, THALO], BF16))
        sb_blend = E(nc.sbuf_tensor("sb_blend", [128, 20], F32))
        sb_w4m = E(nc.sbuf_tensor("sb_w4m", [128, 512], BF16))
        sb_rmat = E(nc.sbuf_tensor("sb_rmat", [128, 128], BF16))
        sb_id = E(nc.sbuf_tensor("sb_id", [128, 128], BF16))
        xnt = E(nc.sbuf_tensor("xnt", [128, NG * TOWN], BF16))
        krt = E(nc.sbuf_tensor("krt", [128, THALO], BF16))
        v_sb = E(nc.sbuf_tensor("v_sb", [128, 12 * DH], BF16))

        pbank = [E(nc.psum_tensor(f"pb{i}", [128, 512], F32)) for i in range(8)]

        def bankbf(i):
            return pbank[i][:, :].bitcast(BF16)

        fb_v = E(nc.sbuf_tensor("fb_v", [1, 2048], F32))
        fb_s = E(nc.sbuf_tensor("fb_s", [1, 2048], F32))
        FBUFS = {"vector": (fb_v, [0]), "scalar": (fb_s, [0])}

        def Plan_():
            return Plan(FBUFS)

        s_dx = Sem(st, nc, "s_dx")
        s_dg = Sem(st, nc, "s_dg")
        s_tap = Sem(st, nc, "s_tap")
        s_ch = Sem(st, nc, "s_ch")
        chain_n = [0]

        def chain(v, ins):
            chain_n[0] += 1
            ins.then_inc(s_ch.h, 1)
            v.wait_ge(s_ch.h, chain_n[0])
        s_dy = Sem(st, nc, "s_dy")
        s_dw = Sem(st, nc, "s_dw")
        s_db = Sem(st, nc, "s_db")

        with contextlib.ExitStack() as st0:
            E0 = st0.enter_context
            prep = E0(nc.sbuf_tensor("prep", [128, 2752], F32))
            prepo = E0(nc.sbuf_tensor("prepo", [128, 2752], BF16))
            s_pp = Sem(st0, nc, "s_pp")
            pl = Plan_()

            for dst, src in ((sb_cos, t_cosb), (sb_sin, t_sinb),
                             (sb_blend, t_blend), (sb_w4m, t_w4m),
                             (sb_rmat, t_rmat), (sb_id, t_ident)):
                pl.op("sync", lambda sy, d=dst, s=src: sy.dma_start(d[:, :], s),
                      incs=[(s_dx, 16)])

            for k, (tsrc, tdst) in enumerate(((t_gst, t_gss), (t_ust, t_uss),
                                              (t_dst, t_dss))):
                pl.op("gpsimd", lambda sy, s=tsrc: sy.dma_start(prep[:, :], s),
                      waits=[(s_pp, k)], incs=[(s_dg, 16)])
                pl.op("vector",
                      lambda v: v.tensor_copy(prepo[:, :], prep[:, :]),
                      waits=[(s_dg, s_dg.v), (s_dy, 16 * k)], incs=[(s_pp, 1)],
                      fence=True)
                pl.op("sync",
                      lambda sy, d=tdst: sy.dma_start(
                          d.rearrange("(p c) -> p c", p=128), prepo[:, :]),
                      waits=[(s_pp, s_pp.v)], incs=[(s_dy, 16)])
            pl.op("sync", lambda sy: sy.nop(),
                  waits=[(s_dy, s_dy.v), (s_dx, s_dx.v)])
            pl.run(nc)

        with contextlib.ExitStack() as st1:
            E1 = st1.enter_context
            xnth = E1(nc.sbuf_tensor("xnth", [128, NG * 512], BF16))
            stA = contextlib.ExitStack()
            EA = stA.enter_context
            xst = [EA(nc.sbuf_tensor(f"xst{i}", [128, D], F32)) for i in range(3)]
            xnb = [EA(nc.sbuf_tensor(f"xnb{i}", [128, D], BF16)) for i in range(3)]
            mk1 = lambda n: [EA(nc.sbuf_tensor(f"{n}{i}", [128, 1], F32))
                             for i in range(3)]
            st1v, stnm, stss, stvr, stsd, strs = (mk1("st1v"), mk1("stnm"),
                                                  mk1("stss"), mk1("stvr"),
                                                  mk1("stsd"), mk1("strs"))

            s_dga = [Sem(st1, nc, f"s_dga{i}") for i in range(3)]
            s_nm = Sem(st1, nc, "s_nm")
            s_sq = Sem(st1, nc, "s_sq")
            s_var = Sem(st1, nc, "s_var")
            s_sd = Sem(st1, nc, "s_sd")
            s_xn = Sem(st1, nc, "s_xn")
            s_xtp = Sem(st1, nc, "s_xtp")
            s_xv = Sem(st1, nc, "s_xv")

            def xn_dest(ti):
                if ti < 2:
                    return xnth, 512, ti * 128
                if ti < 10:
                    return xnt, TOWN, (ti - 2) * 128
                return xnth, 512, 256 + (ti - 10) * 128

            pl = Plan_()

            def a_load(ti):
                p = ti % 3
                pl.op("gpsimd",
                      lambda sy, p=p, ti=ti: sy.dma_start(xst[p][:, :], t_xs[ti]),
                      waits=([(s_sq, ti - 2), (s_xn, ti - 2)] if ti >= 3 else []),
                      incs=[(s_dga[p], 16)])

            def a_stats1(ti):
                p = ti % 3
                scr = xnb[ti % 3]

                def a_red(v, p=p):
                    chain(v, v.reduce_sum(st1v[p][:, :], xst[p][:, :], AX.X))
                    return v.tensor_scalar_mul(stnm[p][:, :], st1v[p][:, :],
                                               -1.0 / D)
                pl.op("vector", a_red,
                      waits=[(s_dga[p], 16 * (ti // 3 + 1)),
                             (s_sq, ti - 2)],
                      incs=[(s_nm, 1)], fence=True)
                pl.op("scalar",
                      lambda s, p=p, scr=scr: s.activation(
                          scr[:, :], xst[p][:, :],
                          AF.Square, bias=stnm[p][:, :], scale=1.0,
                          accum_out=stss[p][:, :]),
                      waits=[(s_nm, s_nm.v), (s_var, ti - 3),
                             (s_xtp, 4 * (ti - 2))],
                      incs=[(s_sq, 1)], fence=True)

            def a_stats2(ti):
                p = ti % 3
                pl.op("vector",
                      lambda v, p=p: v.tensor_scalar(stvr[p][:, :],
                                                     stss[p][:, :],
                                                     1.0 / D, EPS, ALU.mult,
                                                     ALU.add),
                      waits=[(s_sq, ti + 1)], incs=[(s_var, 1)])
                pl.op("scalar",
                      lambda s, p=p: s.activation(stsd[p][:, :], stvr[p][:, :],
                                                  AF.Sqrt),
                      waits=[(s_var, s_var.v), (s_xn, ti - 3)],
                      incs=[(s_sd, 1)], fence=True)

            def a_apply(ti):
                p = ti % 3

                def a_xn(v, p=p):
                    chain(v, v.reciprocal(strs[p][:, :], stsd[p][:, :]))
                    return v.tensor_scalar(xnb[p][:, :], xst[p][:, :],
                                           stnm[p][:, :], strs[p][:, :],
                                           ALU.add, ALU.mult)
                pl.op("vector", a_xn,
                      waits=[(s_sd, ti + 1), (s_xtp, 4 * (ti - 1))],
                      incs=[(s_xn, 1)], fence=True)

            def a_trans(ti):
                p = ti % 3
                dstt, span, coff = xn_dest(ti)
                for jb in range(4):
                    fidx = ti * 4 + jb

                    def a_tp(t, p=p, jb=jb):
                        ins = None
                        for j8 in range(8):
                            j = jb * 8 + j8
                            ins = t.transpose(
                                bankbf(jb % 2)[:, j8 * 128:(j8 + 1) * 128],
                                xnb[p][:, j * 128:(j + 1) * 128], sb_id[:, :])
                        return ins
                    pl.op("tensor", a_tp,
                          waits=[(s_xn, ti + 1), (s_xv, fidx - 1)],
                          incs=[(s_xtp, 1)])

                    def a_ev(s, jb=jb, dstt=dstt, span=span, coff=coff):
                        dap = dstt[:, :].rearrange("p (g t) -> p g t", t=span)[
                            :, jb * 8:(jb + 1) * 8, coff:coff + 128]
                        sap = bankbf(jb % 2)[:, 0:1024].rearrange(
                            "p (g t) -> p g t", t=128)
                        return s.activation(dap, sap, AF.Copy, bias=0.0,
                                            scale=1.0)
                    pl.op("scalar", a_ev, waits=[(s_xtp, s_xtp.v)],
                          incs=[(s_xv, 1)])

            a_load(0); a_load(1)
            a_stats1(0)
            for ti in range(12):
                if ti + 2 < 12:
                    a_load(ti + 2)
                if ti + 1 < 12:
                    a_stats1(ti + 1)
                a_stats2(ti)
                a_apply(ti)
                a_trans(ti)
            pl.run(nc)
            stA.close()

            wkv = E1(nc.sbuf_tensor("wkv", [128, 2 * NG * 128], BF16))
            ktv = E1(nc.sbuf_tensor("ktv", [128, THALO], BF16))
            rtmp1 = E1(nc.sbuf_tensor("rtmp1", [128, 512], F32))
            rtmp2 = E1(nc.sbuf_tensor("rtmp2", [128, 512], F32))
            s_kpe = Sem(st1, nc, "s_kpe")
            s_kev = Sem(st1, nc, "s_kev")
            s_rot = Sem(st1, nc, "s_rot")
            s_rp = Sem(st1, nc, "s_rp")
            s_vtp = Sem(st1, nc, "s_vtp")
            s_vev = Sem(st1, nc, "s_vev")

            def kv_rhs(c0, grp):
                if c0 < 256:
                    return xnth[:, grp * 512 + c0: grp * 512 + c0 + 256]
                if c0 < 1280:
                    o = c0 - 256
                    return xnt[:, grp * TOWN + o: grp * TOWN + o + 256]
                o = c0 - 1024
                return xnth[:, grp * 512 + o: grp * 512 + o + 256]

            pl = Plan_()
            pl.op("gpsimd", lambda g: g.dma_start(wkv[:, 0:NG * 128], t_wkt),
                  incs=[(s_dw, 16)])
            pl.op("gpsimd", lambda g: g.dma_start(wkv[:, NG * 128:], t_wvt),
                  incs=[(s_dw, 16)])
            for which in range(2):
                for b in range(3):
                    def bk_mm(t, which=which, b=b):
                        ins = None
                        for sub in range(2):
                            c0 = b * 512 + sub * 256
                            for j in range(NG):
                                ins = t.matmul(
                                    pbank[b][:, sub * 256:(sub + 1) * 256],
                                    wkv[:, which * NG * 128 + j * 128:
                                        which * NG * 128 + (j + 1) * 128],
                                    kv_rhs(c0, j),
                                    start=(sub == 0 and j == 0),
                                    stop=(sub == 1 and j == NG - 1))
                        return ins
                    waits = [(s_dw, s_dw.v)] if (which == 0 and b == 0) else []
                    if which == 1:
                        waits = [(s_kev, b + 1)]
                    pl.op("tensor", bk_mm, waits=waits, incs=[(s_kpe, 1)])

                    ev_waits = [(s_kpe, s_kpe.v)]
                    if which == 1:
                        ev_waits.append((s_rp, b + 1))
                    pl.op("scalar",
                          lambda s, b=b: s.activation(
                              ktv[:, b * 512:(b + 1) * 512], pbank[b][:, :],
                              AF.Copy, bias=0.0, scale=1.0),
                          waits=ev_waits, incs=[(s_kev, 1)], fence=True)

                if which == 0:
                    for b in range(3):
                        pl.op("tensor",
                              lambda t, b=b: t.matmul(
                                  pbank[3 + b][:, :], sb_rmat[:, :],
                                  ktv[:, b * 512:(b + 1) * 512],
                                  start=True, stop=True),
                              waits=[(s_kev, b + 1)], incs=[(s_rot, 1)])

                        def b_rope(v, b=b):
                            sl = slice(b * 512, (b + 1) * 512)
                            v.tensor_mul(rtmp1[:, :], ktv[:, sl], sb_cos[:, sl])
                            chain(v, v.tensor_mul(rtmp2[:, :],
                                                  pbank[3 + b][:, :],
                                                  sb_sin[:, sl]))
                            return v.tensor_add(krt[:, sl], rtmp1[:, :],
                                                rtmp2[:, :])
                        pl.op("vector", b_rope,
                              waits=[(s_rot, s_rot.v), (s_rp, b)],
                              incs=[(s_rp, 1)])

            for fb in range(3):
                def b_vt(t, fb=fb):
                    ins = None
                    for k in range(4):
                        tc = fb * 4 + k
                        ins = t.transpose(
                            bankbf(6 + fb % 2)[:, k * 128:(k + 1) * 128],
                            ktv[:, tc * 128:(tc + 1) * 128], sb_id[:, :])
                    return ins
                pl.op("tensor", b_vt,
                      waits=[(s_kev, 3 + fb + 1), (s_vev, fb - 1)],
                      incs=[(s_vtp, 1)])
                pl.op("scalar",
                      lambda s, fb=fb: s.activation(
                          v_sb[:, fb * 512:(fb + 1) * 512],
                          bankbf(6 + fb % 2)[:, 0:512], AF.Copy, bias=0.0,
                          scale=1.0),
                      waits=[(s_vtp, s_vtp.v)], incs=[(s_vev, 1)])

            if "xnt" in taps:
                pl.op("sync", lambda sy: sy.dma_start(taps["xnt"], xnt[:, :]),
                      waits=[(s_xv, 48)], incs=[(s_dy, 16)])
            if "krt" in taps:
                pl.op("sync", lambda sy: sy.dma_start(taps["krt"], krt[:, :]),
                      waits=[(s_rp, 3)], incs=[(s_dy, 16)])
            if "vsb" in taps:
                pl.op("sync", lambda sy: sy.dma_start(taps["vsb"], v_sb[:, :]),
                      waits=[(s_vev, 3)], incs=[(s_dy, 16)])
            pl.op("sync", lambda sy: sy.nop(), waits=[(s_dy, s_dy.v)])
            pl.run(nc)

        with contextlib.ExitStack() as st2:
            E2 = st2.enter_context
            at_sb = E2(nc.sbuf_tensor("at_sb", [128, NH * TOWN], BF16))

            with contextlib.ExitStack() as st2c:
                E2c = st2c.enter_context
                wq = [E2c(nc.sbuf_tensor(f"wq{i}", [128, D], BF16))
                      for i in range(2)]
                qt_sb = [E2c(nc.sbuf_tensor(f"qt_sb{i}", [128, TOWN], BF16))
                         for i in range(2)]
                qr_sb = [E2c(nc.sbuf_tensor(f"qr_sb{i}", [128, TOWN], BF16))
                         for i in range(2)]
                p_sb = [E2c(nc.sbuf_tensor(f"p_sb{i}", [128, 2048], BF16))
                        for i in range(2)]
                pt_sb = [E2c(nc.sbuf_tensor(f"pt{i}", [128, 2048], BF16))
                         for i in range(3)]
                qtmp1 = E2c(nc.sbuf_tensor("qtmp1", [128, 512], F32))
                qtmp2 = E2c(nc.sbuf_tensor("qtmp2", [128, 512], F32))
                rs_sb = E2c(nc.sbuf_tensor("rs_sb", [128, 8], F32))
                rc_sb = E2c(nc.sbuf_tensor("rc_sb", [128, 8], F32))
                sv_sb = E2c(nc.sbuf_tensor("sv_sb", [128, 8], F32))

                s_wqd = [Sem(st2c, nc, f"s_wqd{i}") for i in range(2)]
                s_qpe = Sem(st2c, nc, "s_qpe")
                s_qev = Sem(st2c, nc, "s_qev")
                s_rtq = Sem(st2c, nc, "s_rtq")
                s_qr = Sem(st2c, nc, "s_qr")
                s_sc = Sem(st2c, nc, "s_sc")
                s_sg = Sem(st2c, nc, "s_sg")
                s_ps = Sem(st2c, nc, "s_ps")
                s_pt = Sem(st2c, nc, "s_pt")
                s_pv_a = Sem(st2c, nc, "s_pv_a")
                s_pv_v = Sem(st2c, nc, "s_pv_v")
                s_av = Sem(st2c, nc, "s_av")
                s_ate = Sem(st2c, nc, "s_ate")

                W_QOFF = [0, 0, 256, 512, 768]
                W_NQC = [2, 4, 4, 4, 2]
                PT_SLOT = [0, 1, 2, 0, 1]
                AV_CONTRIB = [
                    [(1, 0, 512, 0), (0, 0, 256, 0), (2, 0, 256, 256)],
                    [(3, 0, 512, 0), (2, 256, 256, 0), (4, 0, 256, 256)],
                ]

                pl = Plan_()
                sg_i = 0
                pt_i = 0
                pchunk_free = {(s, c): 0 for s in range(2)
                               for c in range(4)}
                slot_free = {0: 0, 1: 0, 2: 0}
                tbank_free = [(s_pv_a, 0), (s_pv_v, 0)]
                sc_after_head = {-1: 0, 0: 0}
                rope_val = {}
                ps_of_chunk = {}
                pv_after_win = {}

                W_QOFF = [0, 0, 256, 512, 768]
                W_NQC = [2, 4, 4, 4, 2]
                PT_SLOT = [0, 1, 2, 0, 1]
                AV_CONTRIB = [
                    [(1, 0, 512, 0), (0, 0, 256, 0), (2, 0, 256, 256)],
                    [(3, 0, 512, 0), (2, 256, 256, 0), (4, 0, 256, 256)],
                ]

                def wq_load(h):
                    par = h % 2
                    pl.op("gpsimd",
                          lambda g, par=par, h=h: g.dma_start(wq[par][:, :],
                                                              t_wqt[h]),
                          waits=[(s_qpe, 2 * (h - 1))],
                          incs=[(s_wqd[par], 16)])

                def q_proj(h, ts):
                    par = h % 2

                    def c_q(t, par=par, ts=ts):
                        ins = None
                        for j in range(NG):
                            ins = t.matmul(
                                pbank[ts][:, :],
                                wq[par][:, j * 128:(j + 1) * 128],
                                xnt[:, j * TOWN + ts * 512:
                                    j * TOWN + ts * 512 + 512],
                                start=(j == 0), stop=(j == NG - 1))
                        return ins
                    pl.op("tensor", c_q,
                          waits=[(s_wqd[par], 16 * (h // 2 + 1)),
                                 (s_qev, 2 * (h - 1) + ts + 1)],
                          incs=[(s_qpe, 1)])
                    pl.op("scalar",
                          lambda s, ts=ts, par=par: s.activation(
                              qt_sb[par][:, ts * 512:(ts + 1) * 512],
                              pbank[ts][:, :], AF.Copy, bias=0.0, scale=1.0),
                          waits=[(s_qpe, s_qpe.v),
                                 (s_qr, rope_val.get(h - 2, 0) if ts == 0
                                  else 0)],
                          incs=[(s_qev, 1)], fence=True)

                def q_rot_rope(h):
                    par = h % 2
                    for ts in range(2):
                        fs, fv = tbank_free[ts]
                        pl.op("tensor",
                              lambda t, ts=ts, par=par: t.matmul(
                                  pbank[4 + ts][:, :], sb_rmat[:, :],
                                  qt_sb[par][:, ts * 512:(ts + 1) * 512],
                                  start=True, stop=True),
                              waits=[(s_qev, 2 * h + 2), (fs, fv)],
                              incs=[(s_rtq, 1)])

                    def c_rope(v, par=par):
                        ins = None
                        for ts in range(2):
                            sl = slice(ts * 512, (ts + 1) * 512)
                            cs = slice(256 + ts * 512, 256 + (ts + 1) * 512)
                            v.tensor_mul(qtmp1[:, :], qt_sb[par][:, sl],
                                         sb_cos[:, cs])
                            chain(v, v.tensor_mul(qtmp2[:, :],
                                                  pbank[4 + ts][:, :],
                                                  sb_sin[:, cs]))
                            ins = v.tensor_add(qr_sb[par][:, sl], qtmp1[:, :],
                                               qtmp2[:, :])
                            if ts == 0:
                                chain(v, ins)
                        return ins
                    pl.op("vector", c_rope,
                          waits=[(s_rtq, s_rtq.v),
                                 (s_sc, sc_after_head.get(h - 2, 0)),
                                 (s_qr, s_qr.v)],
                          incs=[(s_qr, 1)], fence=True)
                    rope_val[h] = s_qr.v
                    tbank_free[0] = (s_qr, s_qr.v)
                    tbank_free[1] = (s_qr, s_qr.v)

                def win_scores(h, w):
                    nonlocal sg_i
                    par = h % 2
                    nqc = W_NQC[w]
                    for c in range(nqc):
                        bki = 2 + (sg_i % 2)
                        pl.op("tensor",
                              lambda t, w=w, c=c, bki=bki, par=par:
                              t.matmul(pbank[bki][:, :],
                                       qr_sb[par][:, W_QOFF[w] + c * 128:
                                                  W_QOFF[w] + c * 128 + 128],
                                       krt[:, 256 * w:256 * w + 512],
                                       start=True, stop=True),
                              waits=[(s_qr, rope_val[h]), (s_sg, sg_i - 1)],
                              incs=[(s_sc, 1)])
                        pw = p_sb[w % 2]
                        rcol = (w % 2) * 4 + c
                        pl.op("scalar",
                              lambda s, c=c, bki=bki, pw=pw, rcol=rcol:
                              s.activation(pw[:, c * 512:(c + 1) * 512],
                                           pbank[bki][:, :], AF.Sigmoid,
                                           bias=0.0, scale=SCALE,
                                           accum_out=rs_sb[:, rcol:rcol + 1]),
                              waits=[(s_sc, s_sc.v),
                                     (s_pt, pchunk_free[(w % 2, c)])],
                              incs=[(s_sg, 1)], fence=True)

                        def c_pp(v, w=w, c=c, pw=pw, rcol=rcol):
                            sl = slice(c * 512, (c + 1) * 512)
                            if w == 4:
                                chain(v, v.tensor_mul(
                                    pw[:, sl], pw[:, sl], sb_w4m[:, :]))
                                chain(v, v.reduce_sum(
                                    rs_sb[:, rcol:rcol + 1], pw[:, sl], AX.X))
                            chain(v, v.reciprocal(rc_sb[:, rcol:rcol + 1],
                                                  rs_sb[:, rcol:rcol + 1]))
                            chain(v, v.tensor_mul(
                                sv_sb[:, rcol:rcol + 1], rc_sb[:, rcol:rcol + 1],
                                sb_blend[:, w * 4 + c:w * 4 + c + 1]))
                            return v.tensor_scalar_mul(
                                pw[:, sl], pw[:, sl], sv_sb[:, rcol:rcol + 1])
                        pl.op("vector", c_pp, waits=[(s_sg, s_sg.v)],
                              incs=[(s_ps, 1)], fence=True)
                        ps_of_chunk[(h, w, c)] = s_ps.v
                        sg_i += 1

                def win_transposes(h, w):
                    nonlocal pt_i
                    nqc = W_NQC[w]
                    slot = PT_SLOT[w]
                    for c in range(nqc):
                        bki = 4 + (pt_i % 2)
                        fs, fv = tbank_free[pt_i % 2]

                        def c_tp(t, c=c, bki=bki, pw=p_sb[w % 2]):
                            ins = None
                            for kc in range(4):
                                ins = t.transpose(
                                    bankbf(bki)[:, kc * 128:(kc + 1) * 128],
                                    pw[:, c * 512 + kc * 128:
                                       c * 512 + kc * 128 + 128],
                                    sb_id[:, :])
                            return ins
                        tw = [(s_ps, ps_of_chunk[(h, w, c)]), (fs, fv)]
                        if c == 0:
                            tw.append((s_av, slot_free[slot]))
                        pl.op("tensor", c_tp, waits=tw, incs=[(s_pt, 1)])
                        pchunk_free[(w % 2, c)] = s_pt.v

                        eng = "vector" if pt_i % 2 else "scalar"
                        psem = s_pv_v if pt_i % 2 else s_pv_a

                        def c_te(o, c=c, bki=bki, slot=slot, eng=eng):
                            dap = pt_sb[slot][:, :].rearrange(
                                "p (k q) -> p k q", q=512)[
                                :, :, c * 128:c * 128 + 128]
                            sap = bankbf(bki)[:, 0:512].rearrange(
                                "p (k q) -> p k q", q=128)
                            if eng == "vector":
                                return o.tensor_copy(dap, sap)
                            return o.activation(dap, sap, AF.Copy,
                                                bias=0.0, scale=1.0)
                        pl.op(eng, c_te, waits=[(s_pt, s_pt.v)],
                              incs=[(psem, 1)], fence=True)
                        tbank_free[pt_i % 2] = (psem, psem.v)
                        pt_i += 1
                    pv_after_win[(h, w)] = (s_pv_a.v, s_pv_v.v)

                def av_block(h, bp):
                    needed_a = max(pv_after_win[(h, wv)][0]
                                   for (wv, _, _, _) in AV_CONTRIB[bp])
                    needed_v = max(pv_after_win[(h, wv)][1]
                                   for (wv, _, _, _) in AV_CONTRIB[bp])

                    def c_av(t, bp=bp):
                        first = True
                        ins = None
                        for ci, (w, so, n, do) in enumerate(AV_CONTRIB[bp]):
                            slotw = PT_SLOT[w]
                            for kc in range(4):
                                ins = t.matmul(
                                    pbank[6 + bp][:, do:do + n],
                                    v_sb[:, (2 * w + kc) * 128:
                                         (2 * w + kc + 1) * 128],
                                    pt_sb[slotw][:, kc * 512 + so:
                                                 kc * 512 + so + n],
                                    start=first,
                                    stop=(ci == 2 and kc == 3))
                                first = False
                        return ins
                    pl.op("tensor", c_av,
                          waits=[(s_pv_a, needed_a), (s_pv_v, needed_v),
                                 (s_ate, 2 * (h - 1) + bp + 1)],
                          incs=[(s_av, 1)])
                    for (wv, _, _, _) in AV_CONTRIB[bp]:
                        slot_free[PT_SLOT[wv]] = s_av.v
                    pl.op("vector",
                          lambda v, h=h, bp=bp: v.tensor_copy(
                              at_sb[:, h * TOWN + bp * 512:
                                    h * TOWN + bp * 512 + 512],
                              pbank[6 + bp][:, :]),
                          waits=[(s_av, s_av.v)], incs=[(s_ate, 1)],
                          fence=True)

                wq_load(0)
                q_proj(0, 0)
                q_proj(0, 1)
                q_rot_rope(0)
                for h in range(NH):
                    if h + 1 < NH:
                        wq_load(h + 1)
                    win_scores(h, 0)
                    if h + 1 < NH:
                        q_proj(h + 1, 0)
                    win_scores(h, 1)
                    win_transposes(h, 0)
                    if h + 1 < NH:
                        q_proj(h + 1, 1)
                    win_scores(h, 2)
                    win_transposes(h, 1)
                    win_transposes(h, 2)
                    av_block(h, 0)
                    if h + 1 < NH:
                        q_rot_rope(h + 1)
                    win_scores(h, 3)
                    win_scores(h, 4)
                    win_transposes(h, 3)
                    win_transposes(h, 4)
                    av_block(h, 1)
                    sc_after_head[h] = s_sc.v

                if "at" in taps:
                    pl.op("sync",
                          lambda sy: sy.dma_start(taps["at"], at_sb[:, :]),
                          waits=[(s_ate, s_ate.v)], incs=[(s_dy, 16)])
                    pl.op("sync", lambda sy: sy.nop(), waits=[(s_dy, s_dy.v)])
                pl.run(nc)

            with contextlib.ExitStack() as st2d:
                E2d = st2d.enter_context
                wo = [E2d(nc.sbuf_tensor(f"wo{i}", [128, NG * 256], BF16))
                      for i in range(2)]
                ot_sb = [E2d(nc.sbuf_tensor(f"ot{i}", [128, 256], F32))
                         for i in range(2)]
                s_dwo = [Sem(st2d, nc, f"s_dwo{i}") for i in range(2)]
                s_dpe = Sem(st2d, nc, "s_dpe")
                s_dev = Sem(st2d, nc, "s_dev")
                s_dst2 = [Sem(st2d, nc, f"s_dst2{i}") for i in range(2)]

                pl = Plan_()
                for fs in range(NFS):
                    par = fs % 2
                    pl.op("gpsimd",
                          lambda g, par=par, fs=fs: g.dma_start(wo[par][:, :],
                                                                t_wot[fs]),
                          waits=[(s_dpe, 8 * (fs - 1))],
                          incs=[(s_dwo[par], 16)])
                    for tc in range(8):
                        g = fs * 8 + tc
                        bki = g % 2

                        def d_mm(t, par=par, tc=tc, bki=bki):
                            ins = None
                            for j in range(NG):
                                ins = t.matmul(
                                    pbank[bki][:, 0:256],
                                    at_sb[:, j * TOWN + tc * 128:
                                          j * TOWN + tc * 128 + 128],
                                    wo[par][:, j * 256:(j + 1) * 256],
                                    start=(j == 0), stop=(j == NG - 1))
                            return ins
                        pl.op("tensor", d_mm,
                              waits=[(s_dwo[par], 16 * (fs // 2 + 1)),
                                     (s_dev, g - 1)],
                              incs=[(s_dpe, 1)])
                        pl.op("vector",
                              lambda v, bki=bki, par2=g % 2:
                              v.tensor_copy(ot_sb[par2][:, :],
                                            pbank[bki][:, 0:256]),
                              waits=[(s_dpe, s_dpe.v),
                                     (s_dst2[g % 2], 16 * (g // 2))],
                              incs=[(s_dev, 1)], fence=True)

                        def d_st(sy, tc=tc, fs=fs, par2=g % 2):
                            return sy.dma_start(
                                t_attnb[tc][:, fs * 256:(fs + 1) * 256],
                                ot_sb[par2][:, :])
                        pl.op("sync", d_st, waits=[(s_dev, s_dev.v)],
                              incs=[(s_dst2[g % 2], 16)])
                        if "attnb" in taps:
                            pl.op("sync",
                                  lambda sy, tc=tc, fs=fs, par2=g % 2:
                                  sy.dma_start(
                                      taps["attnb"][tc][:, fs * 256:
                                                        (fs + 1) * 256],
                                      ot_sb[par2][:, :]),
                                  incs=[(s_tap, 16)])
                pl.op("sync", lambda sy: sy.nop(),
                      waits=[(s_dst2[0], s_dst2[0].v), (s_dst2[1], s_dst2[1].v),
                             (s_tap, s_tap.v)])
                pl.run(nc)

        with contextlib.ExitStack() as st3:
            E3 = st3.enter_context
            gq = [E3(nc.sbuf_tensor(f"gq{i}", [128, D], I16)) for i in range(2)]
            uq = [E3(nc.sbuf_tensor(f"uq{i}", [128, D], I16)) for i in range(2)]
            gb = [E3(nc.sbuf_tensor(f"gb{i}", [128, D], BF16)) for i in range(2)]
            ub = [E3(nc.sbuf_tensor(f"ub{i}", [128, D], BF16)) for i in range(2)]
            sg_sb = [E3(nc.sbuf_tensor(f"sg{i}", [128, TOWN], F32))
                     for i in range(2)]
            yt_sb = [E3(nc.sbuf_tensor(f"yt{i}", [128, TOWN], BF16))
                     for i in range(2)]
            s_egq = [Sem(st3, nc, f"s_egq{i}") for i in range(2)]
            s_euq = [Sem(st3, nc, f"s_euq{i}") for i in range(2)]
            s_egb = [Sem(st3, nc, f"s_egb{i}") for i in range(2)]
            s_eub = [Sem(st3, nc, f"s_eub{i}") for i in range(2)]
            s_edq = Sem(st3, nc, "s_edq")
            s_epe = Sem(st3, nc, "s_epe")
            s_esi = Sem(st3, nc, "s_esi")
            s_eyt = Sem(st3, nc, "s_eyt")
            s_dyt = [Sem(st3, nc, f"s_dyt{i}") for i in range(2)]

            pl = Plan_()

            def e_loads(ic):
                par = ic % 2
                pl.op("gpsimd",
                      lambda g, par=par, ic=ic: g.dma_start(gq[par][:, :],
                                                            t_gqt[ic]),
                      waits=[(s_epe, 4 * (ic - 1))], incs=[(s_egq[par], 16)])
                pl.op("gpsimd",
                      lambda g, par=par, ic=ic: g.dma_start(uq[par][:, :],
                                                            t_uqt[ic]),
                      incs=[(s_euq[par], 16)])
                pl.op("gpsimd",
                      lambda g, par=par, ic=ic: g.dma_start(
                          gb[par][:, :],
                          t_gss[ic * D:(ic + 1) * D].partition_broadcast(128)),
                      waits=[(s_edq, 2 * (ic - 1))], incs=[(s_egb[par], 16)])
                pl.op("gpsimd",
                      lambda g, par=par, ic=ic: g.dma_start(
                          ub[par][:, :],
                          t_uss[ic * D:(ic + 1) * D].partition_broadcast(128)),
                      incs=[(s_eub[par], 16)])

            def e_deq(ic):
                par = ic % 2
                nld = 16 * (ic // 2 + 1)
                pl.op("vector",
                      lambda v, par=par: v.scalar_tensor_tensor(
                          gq[par][:, :].bitcast(BF16), gq[par][:, :], -8.0,
                          gb[par][:, :], ALU.add, ALU.mult),
                      waits=[(s_egq[par], nld), (s_egb[par], nld)],
                      incs=[(s_edq, 1)], fence=True)
                pl.op("vector",
                      lambda v, par=par: v.scalar_tensor_tensor(
                          uq[par][:, :].bitcast(BF16), uq[par][:, :], -8.0,
                          ub[par][:, :], ALU.add, ALU.mult),
                      waits=[(s_euq[par], nld), (s_eub[par], nld)],
                      incs=[(s_edq, 1)], fence=True)

            def e_mms(ic):
                par = ic % 2
                for k in range(4):
                    bki = 4 * par + k
                    src_ = gq[par] if k < 2 else uq[par]
                    ts = k % 2
                    wfree = (s_esi,) if k < 2 else (s_eyt,)

                    def e_mm(t, src=src_, ts=ts, bki=bki):
                        ins = None
                        for j in range(NG):
                            ins = t.matmul(
                                pbank[bki][:, :],
                                src[:, j * 128:(j + 1) * 128].bitcast(BF16),
                                xnt[:, j * TOWN + ts * 512:
                                    j * TOWN + ts * 512 + 512],
                                start=(j == 0), stop=(j == NG - 1))
                        return ins
                    pl.op("tensor", e_mm,
                          waits=[(s_edq, 2 * ic + 1 if k < 2 else 2 * ic + 2),
                                 (wfree[0], 2 * (ic - 2) + k % 2 + 1)],
                          incs=[(s_epe, 1)])
                for ts in range(2):
                    pl.op("scalar",
                          lambda s, par=par, ts=ts: s.activation(
                              sg_sb[par][:, ts * 512:(ts + 1) * 512],
                              pbank[4 * par + ts][:, :], AF.Silu, bias=0.0,
                              scale=1.0),
                          waits=[(s_epe, 4 * ic + ts + 1),
                                 (s_eyt, 2 * (ic - 2) + ts + 1)],
                          incs=[(s_esi, 1)], fence=True)

            def e_yt(ic):
                par = ic % 2
                for ts in range(2):
                    pl.op("vector",
                          lambda v, par=par, ts=ts: v.tensor_mul(
                              yt_sb[par][:, ts * 512:(ts + 1) * 512],
                              sg_sb[par][:, ts * 512:(ts + 1) * 512],
                              pbank[4 * par + 2 + ts][:, :]),
                          waits=[(s_esi, 2 * ic + ts + 1),
                                 (s_epe, 4 * ic + 2 + ts + 1),
                                 (s_dyt[ic % 2], 16 * (ic // 2))],
                          incs=[(s_eyt, 1)], fence=True)
                pl.op("sync",
                      lambda sy, par=par, ic=ic: sy.dma_start(t_ytb[ic],
                                                              yt_sb[par][:, :]),
                      waits=[(s_eyt, s_eyt.v)], incs=[(s_dyt[ic % 2], 16)])
                if "ytb" in taps:
                    pl.op("sync",
                          lambda sy, par=par, ic=ic: sy.dma_start(
                              taps["ytb"][ic], yt_sb[par][:, :]),
                          incs=[(s_tap, 16)])

            e_loads(0)
            e_deq(0)
            for ic in range(NIC):
                if ic + 1 < NIC:
                    e_loads(ic + 1)
                    e_deq(ic + 1)
                e_mms(ic)
                e_yt(ic)
            pl.op("sync", lambda sy: sy.nop(),
                  waits=[(s_dyt[0], s_dyt[0].v), (s_dyt[1], s_dyt[1].v),
                         (s_tap, s_tap.v)])
            pl.run(nc)

        with contextlib.ExitStack() as st4:
            E4 = st4.enter_context
            NBUF = 4
            MACROS = []
            for hs in range(NHS):
                MACROS += [(hs, i * 4, 4) for i in range(21)] + [(hs, 84, 2)]
            NMAC = len(MACROS)
            dqb = [E4(nc.sbuf_tensor(f"dqb{i}", [128, 2048], I16))
                   for i in range(NBUF)]
            db = [E4(nc.sbuf_tensor(f"db{i}", [128, 2048], BF16))
                  for i in range(NBUF)]
            ytl = [E4(nc.sbuf_tensor(f"ytl{i}", [128, 4 * TOWN], BF16))
                   for i in range(NBUF)]
            xev = E4(nc.sbuf_tensor("xev", [128, D], F32))
            aev = E4(nc.sbuf_tensor("aev", [128, D], F32))
            o_sb = [E4(nc.sbuf_tensor(f"osb{i}", [128, 512], F32))
                    for i in range(2)]
            s_fdw = [Sem(st4, nc, f"s_fdw{i}") for i in range(NBUF)]
            s_fdb = [Sem(st4, nc, f"s_fdb{i}") for i in range(NBUF)]
            s_fylp = [Sem(st4, nc, f"s_fylp{i}") for i in range(NBUF)]
            s_dyyp = [Sem(st4, nc, f"s_dyyp{i}") for i in range(2)]
            s_fdq = Sem(st4, nc, "s_fdq")
            s_fpe = Sem(st4, nc, "s_fpe")
            s_fev = Sem(st4, nc, "s_fev")

            def consumed(m):
                if m < 0:
                    return 0
                hs, ic0, nm = MACROS[m]
                return hs * NIC + ic0 + nm

            pl = Plan_()

            def f_loads(m):
                hs, ic0, nm = MACROS[m]
                par = m % NBUF
                free = consumed(m - NBUF)
                pl.op("gpsimd",
                      lambda g, par=par, ic0=ic0, nm=nm, hs=hs:
                      g.dma_start(dqb[par][:, 0:nm * 512],
                                  t_dqt[ic0:ic0 + nm, :,
                                        hs * 512:(hs + 1) * 512]
                                  .rearrange("a p h -> p a h")),
                      waits=[(s_fpe, free)], incs=[(s_fdw[par], 16)])
                pl.op("sync",
                      lambda sy, par=par, ic0=ic0, nm=nm, hs=hs:
                      sy.dma_start(ytl[par][:, 0:nm * TOWN],
                                   t_ytb[ic0:ic0 + nm]
                                   .rearrange("a p t -> p a t")),
                      waits=[(s_fpe, free)], incs=[(s_fylp[par], 16)])
                pl.op("gpsimd",
                      lambda g, par=par, ic0=ic0, nm=nm, hs=hs:
                      g.dma_start(
                          db[par][:, 0:nm * 512],
                          t_dss[hs * 44032 + ic0 * 512:
                                hs * 44032 + (ic0 + nm) * 512]
                          .partition_broadcast(128)),
                      waits=[(s_fdq, m - NBUF + 1)], incs=[(s_fdb[par], 16)])

            def f_deq(m):
                hs, ic0, nm = MACROS[m]
                par = m % NBUF
                nmac = 16 * (m // NBUF + 1)
                pl.op("vector",
                      lambda v, par=par, nm=nm: v.scalar_tensor_tensor(
                          dqb[par][:, 0:nm * 512].bitcast(BF16),
                          dqb[par][:, 0:nm * 512], -8.0,
                          db[par][:, 0:nm * 512], ALU.add, ALU.mult),
                      waits=[(s_fdw[par], nmac), (s_fdb[par], nmac)],
                      incs=[(s_fdq, 1)], fence=True)

            def f_mms(m):
                hs, ic0, nm = MACROS[m]
                par = m % NBUF
                for ici in range(nm):
                    ic = ic0 + ici

                    def f_mm(t, ic=ic, ici=ici, par=par):
                        ins = None
                        for tc in range(8):
                            ins = t.matmul(
                                pbank[tc][:, :],
                                ytl[par][:, ici * TOWN + tc * 128:
                                         ici * TOWN + (tc + 1) * 128],
                                dqb[par][:, ici * 512:(ici + 1) * 512]
                                .bitcast(BF16),
                                start=(ic == 0), stop=(ic == NIC - 1))
                        return ins
                    mmw = [(s_fdq, m + 1),
                           (s_fylp[par], 16 * (m // NBUF + 1))]
                    if ic == 0:
                        mmw.append((s_fev, 8 * hs))
                    pl.op("tensor", f_mm, waits=mmw, incs=[(s_fpe, 1)])

            def f_xaev(hs):
                pl.op("gpsimd",
                      lambda g, hs=hs: g.dma_start(
                          xev[:, :],
                          t_xs[2:10, :, hs * 512:(hs + 1) * 512]
                          .rearrange("a p h -> p a h")),
                      waits=[(s_fev, 8 * hs)], incs=[(s_dg, 16)])
                pl.op("gpsimd",
                      lambda g, hs=hs: g.dma_start(
                          aev[:, :],
                          t_attnb[0:8, :, hs * 512:(hs + 1) * 512]
                          .rearrange("a p h -> p a h")),
                      incs=[(s_dg, 16)])

            def f_evacs(hs):
                for tc in range(8):
                    gev = hs * 8 + tc

                    def f_ev(v, tc=tc, par2=gev % 2):
                        chain(v, v.tensor_add(o_sb[par2][:, :], pbank[tc][:, :],
                                              xev[:, tc * 512:(tc + 1) * 512]))
                        return v.tensor_add(o_sb[par2][:, :], o_sb[par2][:, :],
                                            aev[:, tc * 512:(tc + 1) * 512])
                    pl.op("vector", f_ev,
                          waits=[(s_fpe, (hs + 1) * NIC), (s_dg, s_dg.v),
                                 (s_dyyp[gev % 2], 16 * (gev // 2))],
                          incs=[(s_fev, 1)], fence=True)
                    pl.op("sync",
                          lambda sy, tc=tc, hs=hs, par2=gev % 2:
                          sy.dma_start(t_y[tc][:, hs * 512:(hs + 1) * 512],
                                       o_sb[par2][:, :]),
                          waits=[(s_fev, s_fev.v)],
                          incs=[(s_dyyp[gev % 2], 16)])

            f_loads(0)
            f_loads(1)
            f_deq(0)
            for m in range(NMAC):
                hs, ic0, nm = MACROS[m]
                if ic0 == 0:
                    f_xaev(hs)
                if m + 2 < NMAC:
                    f_loads(m + 2)
                if m + 1 < NMAC:
                    f_deq(m + 1)
                f_mms(m)
                if ic0 + nm == NIC:
                    f_evacs(hs)
            pl.op("sync", lambda sy: sy.nop(),
                  waits=[(s_dyyp[0], s_dyyp[0].v), (s_dyyp[1], s_dyyp[1].v)])
            pl.run(nc)

    return nc


def _rope_tables():
    inv = (1.0 / (10000.0 ** (np.arange(0, DH, 2, dtype=np.float32) / DH))
           ).astype(np.float32)
    tpos = np.arange(S, dtype=np.float32)
    f = np.outer(tpos, inv)
    cosf = np.zeros((S, DH), np.float32)
    sinf = np.zeros((S, DH), np.float32)
    cosf[:, 0::2] = np.cos(f)
    cosf[:, 1::2] = np.cos(f)
    sinf[:, 0::2] = np.sin(f)
    sinf[:, 1::2] = np.sin(f)
    return cosf, sinf


def _blend_for_core(c):
    bl = np.zeros((128, 20), np.float32)
    r = np.arange(128, dtype=np.float32)
    for w in range(5):
        W = 4 * c - 1 + w
        if W < 0:
            continue
        if w == 0:
            for cc in range(2):
                oq = cc * 128 + r
                bl[:, w * 4 + cc] = 1.0 - oq / 255.0
        elif w == 4:
            for cc in range(2):
                wl = cc * 128 + r
                bl[:, w * 4 + cc] = wl / 255.0
        else:
            for cc in range(4):
                wl = cc * 128 + r
                if cc < 2:
                    val = np.ones(128, np.float32) if W == 0 else wl / 255.0
                else:
                    val = 1.0 - (wl - 256.0) / 255.0
                bl[:, w * 4 + cc] = val
    return bl


def kernel(L0_attn_x, L0_attn_Wq, L0_attn_Wk, L0_attn_Wv, L0_attn_Wo,
           L0_ffn_gate_q, L0_ffn_gate_s, L0_ffn_up_q, L0_ffn_up_s,
           L0_ffn_down_q, L0_ffn_down_s):
    x = np.ascontiguousarray(np.asarray(L0_attn_x).reshape(S, D),
                             dtype=np.float32)
    Wq = np.asarray(L0_attn_Wq, np.float32)
    Wk = np.asarray(L0_attn_Wk, np.float32)
    Wv = np.asarray(L0_attn_Wv, np.float32)
    Wo = np.asarray(L0_attn_Wo, np.float32)
    gate_q = np.asarray(L0_ffn_gate_q, np.int32)
    up_q = np.asarray(L0_ffn_up_q, np.int32)
    down_q = np.asarray(L0_ffn_down_q, np.int32)
    gate_s = np.asarray(L0_ffn_gate_s, np.float32)
    up_s = np.asarray(L0_ffn_up_s, np.float32)
    down_s = np.asarray(L0_ffn_down_s, np.float32)

    wqt = np.ascontiguousarray(
        Wq.reshape(NH, DH, NG, 128).transpose(0, 3, 2, 1).reshape(NH, 128, D))
    wkt = np.ascontiguousarray(
        Wk.reshape(DH, NG, 128).transpose(2, 1, 0).reshape(128, D))
    wvt = np.ascontiguousarray(
        Wv.reshape(DH, NG, 128).transpose(2, 1, 0).reshape(128, D))
    wot = np.ascontiguousarray(
        Wo.reshape(NFS, 256, NG, 128).transpose(0, 3, 2, 1)
        .reshape(NFS, 128, NG * 256))
    gqt = np.ascontiguousarray(
        gate_q.reshape(NG, 128, NIC, 128).transpose(2, 1, 0, 3)
        .reshape(NIC, 128, D))
    uqt = np.ascontiguousarray(
        up_q.reshape(NG, 128, NIC, 128).transpose(2, 1, 0, 3)
        .reshape(NIC, 128, D))
    dqt = np.ascontiguousarray(down_q.reshape(NIC, 128, D))
    gst = np.ascontiguousarray(
        gate_s.reshape(NG, NIC, 128).transpose(1, 0, 2).reshape(128, 2752))
    ust = np.ascontiguousarray(
        up_s.reshape(NG, NIC, 128).transpose(1, 0, 2).reshape(128, 2752))
    dst = np.ascontiguousarray(
        down_s.reshape(NIC, NHS, 512).transpose(1, 0, 2).reshape(128, 2752))
    rmat = np.zeros((128, 128), BF16NP)
    for i in range(128):
        rmat[i, (i + 64) % 128] = 1
    ident = np.eye(128).astype(BF16NP)
    cosf, sinf = _rope_tables()

    in_maps = []
    for c in range(NCORE):
        lo = 1024 * c - 256
        xs = np.zeros((THALO, D), np.float32)
        a, b = max(0, lo), min(S, lo + THALO)
        xs[a - lo:b - lo] = x[a:b]
        cpad = np.zeros((THALO, DH), np.float32)
        spad = np.zeros((THALO, DH), np.float32)
        cpad[a - lo:b - lo] = cosf[a:b]
        spad[a - lo:b - lo] = sinf[a:b]
        cosb = np.ascontiguousarray(cpad.T).astype(BF16NP)
        sinT = np.ascontiguousarray(spad.T)
        sinT[0:64, :] *= -1.0
        sinb = sinT.astype(BF16NP)
        w4m = np.ones((128, 512), BF16NP)
        if c == NCORE - 1:
            w4m[:, 256:] = 0
        in_maps.append(dict(
            xs=xs.reshape(12, 128, D), wqt=wqt, wkt=wkt, wvt=wvt, wot=wot,
            gqt=gqt, uqt=uqt, dqt=dqt, gst=gst, ust=ust, dst=dst,
            cosb=cosb, sinb=sinb, blend=_blend_for_core(c), w4m=w4m,
            rmat=rmat, ident=ident))

    nc = build_nc()
    res = run_bass_kernel_spmd(nc, in_maps, core_ids=list(range(NCORE)))
    y = np.concatenate(
        [np.asarray(r["y"], np.float32).reshape(TOWN, D)
         for r in res.results], axis=0)
    if DEBUG_TAPS:
        kernel.last_results = res
    return y.reshape(1, S, D)


# version 10 (final)
# speedup vs baseline: 1.1248x; 1.1083x over previous
import contextlib
import math

import numpy as np
import ml_dtypes

import concourse.bass as bass
import concourse.mybir as mybir
from concourse.bass_utils import run_bass_kernel_spmd

F32 = mybir.dt.float32
BF16 = mybir.dt.bfloat16
I32 = mybir.dt.int32
I16 = mybir.dt.int16
AF = mybir.ActivationFunctionType
ALU = mybir.AluOpType
AX = mybir.AxisListType

NCORE = 8
S = 8192
D = 4096
NH = 32
DH = 128
NG = 32
TOWN = 1024
THALO = 1536
I_FFN = 11008
NIC = 86
NHS = 8
NFS = 16
SCALE = 1.0 / math.sqrt(DH)
EPS = 1e-5
BF16NP = ml_dtypes.bfloat16

ENGS = ("sync", "gpsimd", "tensor", "vector", "scalar")
DEBUG_TAPS = []


class Sem:
    def __init__(self, stack, nc, name):
        self.h = stack.enter_context(nc.semaphore(name))
        self.v = 0


class Plan:

    def __init__(self, fence_bufs=None):
        self.items = {e: [] for e in ENGS}
        self.fence_bufs = fence_bufs or {}

    def op(self, eng, emit, waits=(), incs=(), fence=False):
        waits = [(s.h, val) for (s, val) in waits if val > 0]
        incs2 = []
        for (s, n) in incs:
            s.v += n
            incs2.append((s.h, n))
        self.items[eng].append((waits, emit, incs2,
                                self.fence_bufs.get(eng) if fence else None))

    def run(self, nc):
        def replay(obj, recs, eng):
            for waits, emit, incs, fb in recs:
                for h, val in waits:
                    obj.wait_ge(h, val)
                ins = emit(obj)
                if fb is not None:
                    buf, fcnt = fb
                    c = fcnt[0] % 2048
                    fcnt[0] += 1
                    if eng == "vector":
                        ins = obj.memset(buf[0:1, c:c + 1], 0.0)
                    else:
                        ins = obj.memzero(buf[0:1, c:c + 1])
                for h, n in incs:
                    ins.then_inc(h, n)

        with nc.Block() as blk:
            for e in ENGS:
                recs = self.items[e]
                if recs:
                    getattr(blk, e)(
                        lambda obj, recs=recs, e=e: replay(obj, recs, e))


def build_nc():
    nc = bass.Bass("TRN2", target_bir_lowering=False)

    t_xs = nc.dram_tensor("xs", [12, 128, D], F32, kind="ExternalInput").ap()
    t_wqt = nc.dram_tensor("wqt", [NH, 128, D], F32, kind="ExternalInput").ap()
    t_wkt = nc.dram_tensor("wkt", [128, D], F32, kind="ExternalInput").ap()
    t_wvt = nc.dram_tensor("wvt", [128, D], F32, kind="ExternalInput").ap()
    t_wot = nc.dram_tensor("wot", [NFS, 128, NG * 256], F32, kind="ExternalInput").ap()
    t_gqt = nc.dram_tensor("gqt", [NIC, 128, D], I32, kind="ExternalInput").ap()
    t_uqt = nc.dram_tensor("uqt", [NIC, 128, D], I32, kind="ExternalInput").ap()
    t_dqt = nc.dram_tensor("dqt", [NIC, 128, D], I32, kind="ExternalInput").ap()
    t_gst = nc.dram_tensor("gst", [128, 2752], F32, kind="ExternalInput").ap()
    t_ust = nc.dram_tensor("ust", [128, 2752], F32, kind="ExternalInput").ap()
    t_dst = nc.dram_tensor("dst", [128, 2752], F32, kind="ExternalInput").ap()
    t_cosb = nc.dram_tensor("cosb", [128, THALO], BF16, kind="ExternalInput").ap()
    t_sinb = nc.dram_tensor("sinb", [128, THALO], BF16, kind="ExternalInput").ap()
    t_blend = nc.dram_tensor("blend", [128, 20], F32, kind="ExternalInput").ap()
    t_w4m = nc.dram_tensor("w4m", [128, 512], BF16, kind="ExternalInput").ap()
    t_rmat = nc.dram_tensor("rmat", [128, 128], BF16, kind="ExternalInput").ap()
    t_ident = nc.dram_tensor("ident", [128, 128], BF16, kind="ExternalInput").ap()

    t_y = nc.dram_tensor("y", [8, 128, D], F32, kind="ExternalOutput").ap()

    t_attnb = nc.dram_tensor("attnb", [8, 128, D], F32, kind="Internal").ap()
    t_ytb = nc.dram_tensor("ytb", [NIC, 128, TOWN], BF16, kind="Internal").ap()
    t_gss = nc.dram_tensor("gss", [352256], BF16, kind="Internal").ap()
    t_uss = nc.dram_tensor("uss", [352256], BF16, kind="Internal").ap()
    t_dss = nc.dram_tensor("dss", [352256], BF16, kind="Internal").ap()

    taps = {}
    for name, shape, dt in (("xnt", [128, NG * TOWN], BF16),
                            ("krt", [128, THALO], BF16),
                            ("vsb", [128, 12 * DH], BF16),
                            ("at", [128, NH * TOWN], BF16),
                            ("attnb", [8, 128, D], F32),
                            ("ytb", [NIC, 128, TOWN], BF16)):
        if name in DEBUG_TAPS:
            taps[name] = nc.dram_tensor("tap_" + name, shape, dt,
                                        kind="ExternalOutput").ap()

    st = contextlib.ExitStack()
    with st:
        E = st.enter_context

        sb_cos = E(nc.sbuf_tensor("sb_cos", [128, THALO], BF16))
        sb_sin = E(nc.sbuf_tensor("sb_sin", [128, THALO], BF16))
        sb_blend = E(nc.sbuf_tensor("sb_blend", [128, 20], F32))
        sb_w4m = E(nc.sbuf_tensor("sb_w4m", [128, 512], BF16))
        sb_rmat = E(nc.sbuf_tensor("sb_rmat", [128, 128], BF16))
        sb_id = E(nc.sbuf_tensor("sb_id", [128, 128], BF16))
        xnt = E(nc.sbuf_tensor("xnt", [128, NG * TOWN], BF16))
        krt = E(nc.sbuf_tensor("krt", [128, THALO], BF16))
        v_sb = E(nc.sbuf_tensor("v_sb", [128, 12 * DH], BF16))

        pbank = [E(nc.psum_tensor(f"pb{i}", [128, 512], F32)) for i in range(8)]

        def bankbf(i):
            return pbank[i][:, :].bitcast(BF16)

        fb_v = E(nc.sbuf_tensor("fb_v", [1, 2048], F32))
        fb_s = E(nc.sbuf_tensor("fb_s", [1, 2048], F32))
        FBUFS = {"vector": (fb_v, [0]), "scalar": (fb_s, [0])}

        def Plan_():
            return Plan(FBUFS)

        s_dx = Sem(st, nc, "s_dx")
        s_dg = Sem(st, nc, "s_dg")
        s_tap = Sem(st, nc, "s_tap")
        s_ch = Sem(st, nc, "s_ch")
        chain_n = [0]

        def chain(v, ins):
            chain_n[0] += 1
            ins.then_inc(s_ch.h, 1)
            v.wait_ge(s_ch.h, chain_n[0])
        s_dy = Sem(st, nc, "s_dy")
        s_dw = Sem(st, nc, "s_dw")
        s_db = Sem(st, nc, "s_db")

        with contextlib.ExitStack() as st0:
            E0 = st0.enter_context
            prep = E0(nc.sbuf_tensor("prep", [128, 2752], F32))
            prepo = E0(nc.sbuf_tensor("prepo", [128, 2752], BF16))
            s_pp = Sem(st0, nc, "s_pp")
            pl = Plan_()

            for dst, src in ((sb_cos, t_cosb), (sb_sin, t_sinb),
                             (sb_blend, t_blend), (sb_w4m, t_w4m),
                             (sb_rmat, t_rmat), (sb_id, t_ident)):
                pl.op("sync", lambda sy, d=dst, s=src: sy.dma_start(d[:, :], s),
                      incs=[(s_dx, 16)])

            for k, (tsrc, tdst) in enumerate(((t_gst, t_gss), (t_ust, t_uss),
                                              (t_dst, t_dss))):
                pl.op("gpsimd", lambda sy, s=tsrc: sy.dma_start(prep[:, :], s),
                      waits=[(s_pp, k)], incs=[(s_dg, 16)])
                pl.op("vector",
                      lambda v: v.tensor_copy(prepo[:, :], prep[:, :]),
                      waits=[(s_dg, s_dg.v), (s_dy, 16 * k)], incs=[(s_pp, 1)],
                      fence=True)
                pl.op("sync",
                      lambda sy, d=tdst: sy.dma_start(
                          d.rearrange("(p c) -> p c", p=128), prepo[:, :]),
                      waits=[(s_pp, s_pp.v)], incs=[(s_dy, 16)])
            pl.op("sync", lambda sy: sy.nop(),
                  waits=[(s_dy, s_dy.v), (s_dx, s_dx.v)])
            pl.run(nc)

        with contextlib.ExitStack() as st1:
            E1 = st1.enter_context
            xnth = E1(nc.sbuf_tensor("xnth", [128, NG * 512], BF16))
            stA = contextlib.ExitStack()
            EA = stA.enter_context
            xst = [EA(nc.sbuf_tensor(f"xst{i}", [128, D], F32)) for i in range(3)]
            xnb = [EA(nc.sbuf_tensor(f"xnb{i}", [128, D], BF16)) for i in range(3)]
            mk1 = lambda n: [EA(nc.sbuf_tensor(f"{n}{i}", [128, 1], F32))
                             for i in range(3)]
            st1v, stnm, stss, stvr, stsd, strs = (mk1("st1v"), mk1("stnm"),
                                                  mk1("stss"), mk1("stvr"),
                                                  mk1("stsd"), mk1("strs"))

            s_dga = [Sem(st1, nc, f"s_dga{i}") for i in range(3)]
            s_nm = Sem(st1, nc, "s_nm")
            s_sq = Sem(st1, nc, "s_sq")
            s_var = Sem(st1, nc, "s_var")
            s_sd = Sem(st1, nc, "s_sd")
            s_xn = Sem(st1, nc, "s_xn")
            s_xtp = Sem(st1, nc, "s_xtp")
            s_xv = Sem(st1, nc, "s_xv")

            def xn_dest(ti):
                if ti < 2:
                    return xnth, 512, ti * 128
                if ti < 10:
                    return xnt, TOWN, (ti - 2) * 128
                return xnth, 512, 256 + (ti - 10) * 128

            pl = Plan_()

            def a_load(ti):
                p = ti % 3
                pl.op("gpsimd",
                      lambda sy, p=p, ti=ti: sy.dma_start(xst[p][:, :], t_xs[ti]),
                      waits=([(s_sq, ti - 2), (s_xn, ti - 2)] if ti >= 3 else []),
                      incs=[(s_dga[p], 16)])

            def a_stats1(ti):
                p = ti % 3
                scr = xnb[ti % 3]

                def a_red(v, p=p):
                    chain(v, v.reduce_sum(st1v[p][:, :], xst[p][:, :], AX.X))
                    return v.tensor_scalar_mul(stnm[p][:, :], st1v[p][:, :],
                                               -1.0 / D)
                pl.op("vector", a_red,
                      waits=[(s_dga[p], 16 * (ti // 3 + 1)),
                             (s_sq, ti - 2)],
                      incs=[(s_nm, 1)], fence=True)
                pl.op("scalar",
                      lambda s, p=p, scr=scr: s.activation(
                          scr[:, :], xst[p][:, :],
                          AF.Square, bias=stnm[p][:, :], scale=1.0,
                          accum_out=stss[p][:, :]),
                      waits=[(s_nm, s_nm.v), (s_var, ti - 3),
                             (s_xtp, 4 * (ti - 2))],
                      incs=[(s_sq, 1)], fence=True)

            def a_stats2(ti):
                p = ti % 3
                pl.op("vector",
                      lambda v, p=p: v.tensor_scalar(stvr[p][:, :],
                                                     stss[p][:, :],
                                                     1.0 / D, EPS, ALU.mult,
                                                     ALU.add),
                      waits=[(s_sq, ti + 1)], incs=[(s_var, 1)])
                pl.op("scalar",
                      lambda s, p=p: s.activation(stsd[p][:, :], stvr[p][:, :],
                                                  AF.Sqrt),
                      waits=[(s_var, s_var.v), (s_xn, ti - 3)],
                      incs=[(s_sd, 1)], fence=True)

            def a_apply(ti):
                p = ti % 3

                def a_xn(v, p=p):
                    chain(v, v.reciprocal(strs[p][:, :], stsd[p][:, :]))
                    return v.tensor_scalar(xnb[p][:, :], xst[p][:, :],
                                           stnm[p][:, :], strs[p][:, :],
                                           ALU.add, ALU.mult)
                pl.op("vector", a_xn,
                      waits=[(s_sd, ti + 1), (s_xtp, 4 * (ti - 1))],
                      incs=[(s_xn, 1)], fence=True)

            def a_trans(ti):
                p = ti % 3
                dstt, span, coff = xn_dest(ti)
                for jb in range(4):
                    fidx = ti * 4 + jb

                    def a_tp(t, p=p, jb=jb):
                        ins = None
                        for j8 in range(8):
                            j = jb * 8 + j8
                            ins = t.transpose(
                                bankbf(jb % 2)[:, j8 * 128:(j8 + 1) * 128],
                                xnb[p][:, j * 128:(j + 1) * 128], sb_id[:, :])
                        return ins
                    pl.op("tensor", a_tp,
                          waits=[(s_xn, ti + 1), (s_xv, fidx - 1)],
                          incs=[(s_xtp, 1)])

                    def a_ev(s, jb=jb, dstt=dstt, span=span, coff=coff):
                        dap = dstt[:, :].rearrange("p (g t) -> p g t", t=span)[
                            :, jb * 8:(jb + 1) * 8, coff:coff + 128]
                        sap = bankbf(jb % 2)[:, 0:1024].rearrange(
                            "p (g t) -> p g t", t=128)
                        return s.activation(dap, sap, AF.Copy, bias=0.0,
                                            scale=1.0)
                    pl.op("scalar", a_ev, waits=[(s_xtp, s_xtp.v)],
                          incs=[(s_xv, 1)])

            a_load(0); a_load(1)
            a_stats1(0)
            for ti in range(12):
                if ti + 2 < 12:
                    a_load(ti + 2)
                if ti + 1 < 12:
                    a_stats1(ti + 1)
                a_stats2(ti)
                a_apply(ti)
                a_trans(ti)
            pl.run(nc)
            stA.close()

            wkv = E1(nc.sbuf_tensor("wkv", [128, 2 * NG * 128], BF16))
            ktv = E1(nc.sbuf_tensor("ktv", [128, THALO], BF16))
            rtmp1 = E1(nc.sbuf_tensor("rtmp1", [128, 512], F32))
            rtmp2 = E1(nc.sbuf_tensor("rtmp2", [128, 512], F32))
            s_kpe = Sem(st1, nc, "s_kpe")
            s_kev = Sem(st1, nc, "s_kev")
            s_rot = Sem(st1, nc, "s_rot")
            s_rp = Sem(st1, nc, "s_rp")
            s_vtp = Sem(st1, nc, "s_vtp")
            s_vev = Sem(st1, nc, "s_vev")

            def kv_rhs(c0, grp):
                if c0 < 256:
                    return xnth[:, grp * 512 + c0: grp * 512 + c0 + 256]
                if c0 < 1280:
                    o = c0 - 256
                    return xnt[:, grp * TOWN + o: grp * TOWN + o + 256]
                o = c0 - 1024
                return xnth[:, grp * 512 + o: grp * 512 + o + 256]

            pl = Plan_()
            pl.op("gpsimd", lambda g: g.dma_start(wkv[:, 0:NG * 128], t_wkt),
                  incs=[(s_dw, 16)])
            pl.op("gpsimd", lambda g: g.dma_start(wkv[:, NG * 128:], t_wvt),
                  incs=[(s_dw, 16)])
            for which in range(2):
                for b in range(3):
                    def bk_mm(t, which=which, b=b):
                        ins = None
                        for sub in range(2):
                            c0 = b * 512 + sub * 256
                            for j in range(NG):
                                ins = t.matmul(
                                    pbank[b][:, sub * 256:(sub + 1) * 256],
                                    wkv[:, which * NG * 128 + j * 128:
                                        which * NG * 128 + (j + 1) * 128],
                                    kv_rhs(c0, j),
                                    start=(sub == 0 and j == 0),
                                    stop=(sub == 1 and j == NG - 1))
                        return ins
                    waits = [(s_dw, s_dw.v)] if (which == 0 and b == 0) else []
                    if which == 1:
                        waits = [(s_kev, b + 1)]
                    pl.op("tensor", bk_mm, waits=waits, incs=[(s_kpe, 1)])

                    ev_waits = [(s_kpe, s_kpe.v)]
                    if which == 1:
                        ev_waits.append((s_rp, b + 1))
                    pl.op("scalar",
                          lambda s, b=b: s.activation(
                              ktv[:, b * 512:(b + 1) * 512], pbank[b][:, :],
                              AF.Copy, bias=0.0, scale=1.0),
                          waits=ev_waits, incs=[(s_kev, 1)], fence=True)

                if which == 0:
                    for b in range(3):
                        pl.op("tensor",
                              lambda t, b=b: t.matmul(
                                  pbank[3 + b][:, :], sb_rmat[:, :],
                                  ktv[:, b * 512:(b + 1) * 512],
                                  start=True, stop=True),
                              waits=[(s_kev, b + 1)], incs=[(s_rot, 1)])

                        def b_rope(v, b=b):
                            sl = slice(b * 512, (b + 1) * 512)
                            v.tensor_mul(rtmp1[:, :], ktv[:, sl], sb_cos[:, sl])
                            chain(v, v.tensor_mul(rtmp2[:, :],
                                                  pbank[3 + b][:, :],
                                                  sb_sin[:, sl]))
                            return v.tensor_add(krt[:, sl], rtmp1[:, :],
                                                rtmp2[:, :])
                        pl.op("vector", b_rope,
                              waits=[(s_rot, s_rot.v), (s_rp, b)],
                              incs=[(s_rp, 1)])

            for fb in range(3):
                def b_vt(t, fb=fb):
                    ins = None
                    for k in range(4):
                        tc = fb * 4 + k
                        ins = t.transpose(
                            bankbf(6 + fb % 2)[:, k * 128:(k + 1) * 128],
                            ktv[:, tc * 128:(tc + 1) * 128], sb_id[:, :])
                    return ins
                pl.op("tensor", b_vt,
                      waits=[(s_kev, 3 + fb + 1), (s_vev, fb - 1)],
                      incs=[(s_vtp, 1)])
                pl.op("scalar",
                      lambda s, fb=fb: s.activation(
                          v_sb[:, fb * 512:(fb + 1) * 512],
                          bankbf(6 + fb % 2)[:, 0:512], AF.Copy, bias=0.0,
                          scale=1.0),
                      waits=[(s_vtp, s_vtp.v)], incs=[(s_vev, 1)])

            if "xnt" in taps:
                pl.op("sync", lambda sy: sy.dma_start(taps["xnt"], xnt[:, :]),
                      waits=[(s_xv, 48)], incs=[(s_dy, 16)])
            if "krt" in taps:
                pl.op("sync", lambda sy: sy.dma_start(taps["krt"], krt[:, :]),
                      waits=[(s_rp, 3)], incs=[(s_dy, 16)])
            if "vsb" in taps:
                pl.op("sync", lambda sy: sy.dma_start(taps["vsb"], v_sb[:, :]),
                      waits=[(s_vev, 3)], incs=[(s_dy, 16)])
            pl.op("sync", lambda sy: sy.nop(), waits=[(s_dy, s_dy.v)])
            pl.run(nc)

        with contextlib.ExitStack() as st2:
            E2 = st2.enter_context
            at_sb = E2(nc.sbuf_tensor("at_sb", [128, NH * TOWN], BF16))

            with contextlib.ExitStack() as st2c:
                E2c = st2c.enter_context
                wq = [E2c(nc.sbuf_tensor(f"wq{i}", [128, D], BF16))
                      for i in range(2)]
                qt_sb = [E2c(nc.sbuf_tensor(f"qt_sb{i}", [128, TOWN], BF16))
                         for i in range(2)]
                qr_sb = [E2c(nc.sbuf_tensor(f"qr_sb{i}", [128, TOWN], BF16))
                         for i in range(2)]
                p_sb = [E2c(nc.sbuf_tensor(f"p_sb{i}", [128, 2048], BF16))
                        for i in range(2)]
                pt_sb = [E2c(nc.sbuf_tensor(f"pt{i}", [128, 2048], BF16))
                         for i in range(3)]
                qtmp1 = E2c(nc.sbuf_tensor("qtmp1", [128, 512], F32))
                qtmp2 = E2c(nc.sbuf_tensor("qtmp2", [128, 512], F32))
                rs_sb = E2c(nc.sbuf_tensor("rs_sb", [128, 8], F32))
                rc_sb = E2c(nc.sbuf_tensor("rc_sb", [128, 8], F32))
                sv_sb = E2c(nc.sbuf_tensor("sv_sb", [128, 8], F32))

                s_wqd = [Sem(st2c, nc, f"s_wqd{i}") for i in range(2)]
                s_qpe = Sem(st2c, nc, "s_qpe")
                s_qev = Sem(st2c, nc, "s_qev")
                s_rtq = Sem(st2c, nc, "s_rtq")
                s_qr = Sem(st2c, nc, "s_qr")
                s_sc = Sem(st2c, nc, "s_sc")
                s_sg = Sem(st2c, nc, "s_sg")
                s_ps = Sem(st2c, nc, "s_ps")
                s_pt = Sem(st2c, nc, "s_pt")
                s_pv_a = Sem(st2c, nc, "s_pv_a")
                s_pv_v = Sem(st2c, nc, "s_pv_v")
                s_av = Sem(st2c, nc, "s_av")
                s_ate = Sem(st2c, nc, "s_ate")

                W_QOFF = [0, 0, 256, 512, 768]
                W_NQC = [2, 4, 4, 4, 2]
                PT_SLOT = [0, 1, 2, 0, 1]
                AV_CONTRIB = [
                    [(1, 0, 512, 0), (0, 0, 256, 0), (2, 0, 256, 256)],
                    [(3, 0, 512, 0), (2, 256, 256, 0), (4, 0, 256, 256)],
                ]

                pl = Plan_()
                sg_i = 0
                pt_i = 0
                pchunk_free = {(s, c): 0 for s in range(2)
                               for c in range(4)}
                slot_free = {0: 0, 1: 0, 2: 0}
                tbank_free = [(s_pv_a, 0), (s_pv_v, 0)]
                sc_after_head = {-1: 0, 0: 0}
                rope_val = {}
                ps_of_chunk = {}
                pv_after_win = {}

                W_QOFF = [0, 0, 256, 512, 768]
                W_NQC = [2, 4, 4, 4, 2]
                PT_SLOT = [0, 1, 2, 0, 1]
                AV_CONTRIB = [
                    [(1, 0, 512, 0), (0, 0, 256, 0), (2, 0, 256, 256)],
                    [(3, 0, 512, 0), (2, 256, 256, 0), (4, 0, 256, 256)],
                ]

                def wq_load(h):
                    par = h % 2
                    pl.op("gpsimd",
                          lambda g, par=par, h=h: g.dma_start(wq[par][:, :],
                                                              t_wqt[h]),
                          waits=[(s_qpe, 2 * (h - 1))],
                          incs=[(s_wqd[par], 16)])

                def q_proj(h, ts):
                    par = h % 2

                    def c_q(t, par=par, ts=ts):
                        ins = None
                        for j in range(NG):
                            ins = t.matmul(
                                pbank[ts][:, :],
                                wq[par][:, j * 128:(j + 1) * 128],
                                xnt[:, j * TOWN + ts * 512:
                                    j * TOWN + ts * 512 + 512],
                                start=(j == 0), stop=(j == NG - 1))
                        return ins
                    pl.op("tensor", c_q,
                          waits=[(s_wqd[par], 16 * (h // 2 + 1)),
                                 (s_qev, 2 * (h - 1) + ts + 1)],
                          incs=[(s_qpe, 1)])
                    pl.op("scalar",
                          lambda s, ts=ts, par=par: s.activation(
                              qt_sb[par][:, ts * 512:(ts + 1) * 512],
                              pbank[ts][:, :], AF.Copy, bias=0.0, scale=1.0),
                          waits=[(s_qpe, s_qpe.v),
                                 (s_qr, rope_val.get(h - 2, 0) if ts == 0
                                  else 0)],
                          incs=[(s_qev, 1)], fence=True)

                def q_rot_rope(h):
                    par = h % 2
                    for ts in range(2):
                        fs, fv = tbank_free[ts]
                        pl.op("tensor",
                              lambda t, ts=ts, par=par: t.matmul(
                                  pbank[4 + ts][:, :], sb_rmat[:, :],
                                  qt_sb[par][:, ts * 512:(ts + 1) * 512],
                                  start=True, stop=True),
                              waits=[(s_qev, 2 * h + 2), (fs, fv)],
                              incs=[(s_rtq, 1)])

                    def c_rope(v, par=par):
                        ins = None
                        for ts in range(2):
                            sl = slice(ts * 512, (ts + 1) * 512)
                            cs = slice(256 + ts * 512, 256 + (ts + 1) * 512)
                            v.tensor_mul(qtmp1[:, :], qt_sb[par][:, sl],
                                         sb_cos[:, cs])
                            chain(v, v.tensor_mul(qtmp2[:, :],
                                                  pbank[4 + ts][:, :],
                                                  sb_sin[:, cs]))
                            ins = v.tensor_add(qr_sb[par][:, sl], qtmp1[:, :],
                                               qtmp2[:, :])
                            if ts == 0:
                                chain(v, ins)
                        return ins
                    pl.op("vector", c_rope,
                          waits=[(s_rtq, s_rtq.v),
                                 (s_sc, sc_after_head.get(h - 2, 0)),
                                 (s_qr, s_qr.v)],
                          incs=[(s_qr, 1)], fence=True)
                    rope_val[h] = s_qr.v
                    tbank_free[0] = (s_qr, s_qr.v)
                    tbank_free[1] = (s_qr, s_qr.v)

                def win_scores(h, w):
                    nonlocal sg_i
                    par = h % 2
                    nqc = W_NQC[w]
                    for c in range(nqc):
                        bki = 2 + (sg_i % 2)
                        pl.op("tensor",
                              lambda t, w=w, c=c, bki=bki, par=par:
                              t.matmul(pbank[bki][:, :],
                                       qr_sb[par][:, W_QOFF[w] + c * 128:
                                                  W_QOFF[w] + c * 128 + 128],
                                       krt[:, 256 * w:256 * w + 512],
                                       start=True, stop=True),
                              waits=[(s_qr, rope_val[h]), (s_sg, sg_i - 1)],
                              incs=[(s_sc, 1)])
                        pw = p_sb[w % 2]
                        rcol = (w % 2) * 4 + c
                        pl.op("scalar",
                              lambda s, c=c, bki=bki, pw=pw, rcol=rcol:
                              s.activation(pw[:, c * 512:(c + 1) * 512],
                                           pbank[bki][:, :], AF.Sigmoid,
                                           bias=0.0, scale=SCALE,
                                           accum_out=rs_sb[:, rcol:rcol + 1]),
                              waits=[(s_sc, s_sc.v),
                                     (s_pt, pchunk_free[(w % 2, c)])],
                              incs=[(s_sg, 1)], fence=True)

                        def c_pp(v, w=w, c=c, pw=pw, rcol=rcol):
                            sl = slice(c * 512, (c + 1) * 512)
                            if w == 4:
                                chain(v, v.tensor_mul(
                                    pw[:, sl], pw[:, sl], sb_w4m[:, :]))
                                chain(v, v.reduce_sum(
                                    rs_sb[:, rcol:rcol + 1], pw[:, sl], AX.X))
                            chain(v, v.reciprocal(rc_sb[:, rcol:rcol + 1],
                                                  rs_sb[:, rcol:rcol + 1]))
                            chain(v, v.tensor_mul(
                                sv_sb[:, rcol:rcol + 1], rc_sb[:, rcol:rcol + 1],
                                sb_blend[:, w * 4 + c:w * 4 + c + 1]))
                            return v.tensor_scalar_mul(
                                pw[:, sl], pw[:, sl], sv_sb[:, rcol:rcol + 1])
                        pl.op("vector", c_pp, waits=[(s_sg, s_sg.v)],
                              incs=[(s_ps, 1)], fence=True)
                        ps_of_chunk[(h, w, c)] = s_ps.v
                        sg_i += 1

                def win_transposes(h, w):
                    nonlocal pt_i
                    nqc = W_NQC[w]
                    slot = PT_SLOT[w]
                    for c in range(nqc):
                        bki = 4 + (pt_i % 2)
                        fs, fv = tbank_free[pt_i % 2]

                        def c_tp(t, c=c, bki=bki, pw=p_sb[w % 2]):
                            ins = None
                            for kc in range(4):
                                ins = t.transpose(
                                    bankbf(bki)[:, kc * 128:(kc + 1) * 128],
                                    pw[:, c * 512 + kc * 128:
                                       c * 512 + kc * 128 + 128],
                                    sb_id[:, :])
                            return ins
                        tw = [(s_ps, ps_of_chunk[(h, w, c)]), (fs, fv)]
                        if c == 0:
                            tw.append((s_av, slot_free[slot]))
                        pl.op("tensor", c_tp, waits=tw, incs=[(s_pt, 1)])
                        pchunk_free[(w % 2, c)] = s_pt.v

                        eng = "vector" if pt_i % 2 else "scalar"
                        psem = s_pv_v if pt_i % 2 else s_pv_a

                        def c_te(o, c=c, bki=bki, slot=slot, eng=eng):
                            dap = pt_sb[slot][:, :].rearrange(
                                "p (k q) -> p k q", q=512)[
                                :, :, c * 128:c * 128 + 128]
                            sap = bankbf(bki)[:, 0:512].rearrange(
                                "p (k q) -> p k q", q=128)
                            if eng == "vector":
                                return o.tensor_copy(dap, sap)
                            return o.activation(dap, sap, AF.Copy,
                                                bias=0.0, scale=1.0)
                        pl.op(eng, c_te, waits=[(s_pt, s_pt.v)],
                              incs=[(psem, 1)], fence=True)
                        tbank_free[pt_i % 2] = (psem, psem.v)
                        pt_i += 1
                    pv_after_win[(h, w)] = (s_pv_a.v, s_pv_v.v)

                def av_block(h, bp):
                    needed_a = max(pv_after_win[(h, wv)][0]
                                   for (wv, _, _, _) in AV_CONTRIB[bp])
                    needed_v = max(pv_after_win[(h, wv)][1]
                                   for (wv, _, _, _) in AV_CONTRIB[bp])

                    def c_av(t, bp=bp):
                        first = True
                        ins = None
                        for ci, (w, so, n, do) in enumerate(AV_CONTRIB[bp]):
                            slotw = PT_SLOT[w]
                            for kc in range(4):
                                ins = t.matmul(
                                    pbank[6 + bp][:, do:do + n],
                                    v_sb[:, (2 * w + kc) * 128:
                                         (2 * w + kc + 1) * 128],
                                    pt_sb[slotw][:, kc * 512 + so:
                                                 kc * 512 + so + n],
                                    start=first,
                                    stop=(ci == 2 and kc == 3))
                                first = False
                        return ins
                    pl.op("tensor", c_av,
                          waits=[(s_pv_a, needed_a), (s_pv_v, needed_v),
                                 (s_ate, 2 * (h - 1) + bp + 1)],
                          incs=[(s_av, 1)])
                    for (wv, _, _, _) in AV_CONTRIB[bp]:
                        slot_free[PT_SLOT[wv]] = s_av.v
                    pl.op("vector",
                          lambda v, h=h, bp=bp: v.tensor_copy(
                              at_sb[:, h * TOWN + bp * 512:
                                    h * TOWN + bp * 512 + 512],
                              pbank[6 + bp][:, :]),
                          waits=[(s_av, s_av.v)], incs=[(s_ate, 1)],
                          fence=True)

                wq_load(0)
                q_proj(0, 0)
                q_proj(0, 1)
                q_rot_rope(0)
                for h in range(NH):
                    if h + 1 < NH:
                        wq_load(h + 1)
                    win_scores(h, 0)
                    if h + 1 < NH:
                        q_proj(h + 1, 0)
                    win_scores(h, 1)
                    win_transposes(h, 0)
                    if h + 1 < NH:
                        q_proj(h + 1, 1)
                    win_scores(h, 2)
                    win_transposes(h, 1)
                    win_transposes(h, 2)
                    av_block(h, 0)
                    if h + 1 < NH:
                        q_rot_rope(h + 1)
                    win_scores(h, 3)
                    win_scores(h, 4)
                    win_transposes(h, 3)
                    win_transposes(h, 4)
                    av_block(h, 1)
                    sc_after_head[h] = s_sc.v

                if "at" in taps:
                    pl.op("sync",
                          lambda sy: sy.dma_start(taps["at"], at_sb[:, :]),
                          waits=[(s_ate, s_ate.v)], incs=[(s_dy, 16)])
                    pl.op("sync", lambda sy: sy.nop(), waits=[(s_dy, s_dy.v)])
                pl.run(nc)

            with contextlib.ExitStack() as st2d:
                E2d = st2d.enter_context
                wo = [E2d(nc.sbuf_tensor(f"wo{i}", [128, NG * 256], BF16))
                      for i in range(2)]
                ot_sb = [E2d(nc.sbuf_tensor(f"ot{i}", [128, 256], F32))
                         for i in range(2)]
                s_dwo = [Sem(st2d, nc, f"s_dwo{i}") for i in range(2)]
                s_dpe = Sem(st2d, nc, "s_dpe")
                s_dev = Sem(st2d, nc, "s_dev")
                s_dst2 = [Sem(st2d, nc, f"s_dst2{i}") for i in range(2)]

                pl = Plan_()
                for fs in range(NFS):
                    par = fs % 2
                    pl.op("gpsimd",
                          lambda g, par=par, fs=fs: g.dma_start(wo[par][:, :],
                                                                t_wot[fs]),
                          waits=[(s_dpe, 8 * (fs - 1))],
                          incs=[(s_dwo[par], 16)])
                    for tc in range(8):
                        g = fs * 8 + tc
                        bki = g % 2

                        def d_mm(t, par=par, tc=tc, bki=bki):
                            ins = None
                            for j in range(NG):
                                ins = t.matmul(
                                    pbank[bki][:, 0:256],
                                    at_sb[:, j * TOWN + tc * 128:
                                          j * TOWN + tc * 128 + 128],
                                    wo[par][:, j * 256:(j + 1) * 256],
                                    start=(j == 0), stop=(j == NG - 1))
                            return ins
                        pl.op("tensor", d_mm,
                              waits=[(s_dwo[par], 16 * (fs // 2 + 1)),
                                     (s_dev, g - 1)],
                              incs=[(s_dpe, 1)])
                        pl.op("vector",
                              lambda v, bki=bki, par2=g % 2:
                              v.tensor_copy(ot_sb[par2][:, :],
                                            pbank[bki][:, 0:256]),
                              waits=[(s_dpe, s_dpe.v),
                                     (s_dst2[g % 2], 16 * (g // 2))],
                              incs=[(s_dev, 1)], fence=True)

                        def d_st(sy, tc=tc, fs=fs, par2=g % 2):
                            return sy.dma_start(
                                t_attnb[tc][:, fs * 256:(fs + 1) * 256],
                                ot_sb[par2][:, :])
                        pl.op("sync", d_st, waits=[(s_dev, s_dev.v)],
                              incs=[(s_dst2[g % 2], 16)])
                        if "attnb" in taps:
                            pl.op("sync",
                                  lambda sy, tc=tc, fs=fs, par2=g % 2:
                                  sy.dma_start(
                                      taps["attnb"][tc][:, fs * 256:
                                                        (fs + 1) * 256],
                                      ot_sb[par2][:, :]),
                                  incs=[(s_tap, 16)])
                pl.op("sync", lambda sy: sy.nop(),
                      waits=[(s_dst2[0], s_dst2[0].v), (s_dst2[1], s_dst2[1].v),
                             (s_tap, s_tap.v)])
                pl.run(nc)

        with contextlib.ExitStack() as st3:
            E3 = st3.enter_context
            gq = [E3(nc.sbuf_tensor(f"gq{i}", [128, D], I16)) for i in range(2)]
            uq = [E3(nc.sbuf_tensor(f"uq{i}", [128, D], I16)) for i in range(2)]
            gb = [E3(nc.sbuf_tensor(f"gb{i}", [128, D], BF16)) for i in range(2)]
            ub = [E3(nc.sbuf_tensor(f"ub{i}", [128, D], BF16)) for i in range(2)]
            sg_sb = [E3(nc.sbuf_tensor(f"sg{i}", [128, TOWN], F32))
                     for i in range(2)]
            yt_sb = [E3(nc.sbuf_tensor(f"yt{i}", [128, TOWN], BF16))
                     for i in range(2)]
            s_egq = [Sem(st3, nc, f"s_egq{i}") for i in range(2)]
            s_euq = [Sem(st3, nc, f"s_euq{i}") for i in range(2)]
            s_egb = [Sem(st3, nc, f"s_egb{i}") for i in range(2)]
            s_eub = [Sem(st3, nc, f"s_eub{i}") for i in range(2)]
            s_edq = Sem(st3, nc, "s_edq")
            s_epe = Sem(st3, nc, "s_epe")
            s_esi = Sem(st3, nc, "s_esi")
            s_eyt = Sem(st3, nc, "s_eyt")
            s_dyt = [Sem(st3, nc, f"s_dyt{i}") for i in range(2)]

            pl = Plan_()

            def e_loads(ic):
                par = ic % 2
                pl.op("gpsimd",
                      lambda g, par=par, ic=ic: g.dma_start(gq[par][:, :],
                                                            t_gqt[ic]),
                      waits=[(s_epe, 4 * (ic - 1))], incs=[(s_egq[par], 16)])
                pl.op("gpsimd",
                      lambda g, par=par, ic=ic: g.dma_start(uq[par][:, :],
                                                            t_uqt[ic]),
                      incs=[(s_euq[par], 16)])
                pl.op("gpsimd",
                      lambda g, par=par, ic=ic: g.dma_start(
                          gb[par][:, :],
                          t_gss[ic * D:(ic + 1) * D].partition_broadcast(128)),
                      waits=[(s_edq, 2 * (ic - 1))], incs=[(s_egb[par], 16)])
                pl.op("gpsimd",
                      lambda g, par=par, ic=ic: g.dma_start(
                          ub[par][:, :],
                          t_uss[ic * D:(ic + 1) * D].partition_broadcast(128)),
                      incs=[(s_eub[par], 16)])

            def e_deq(ic):
                par = ic % 2
                nld = 16 * (ic // 2 + 1)
                pl.op("vector",
                      lambda v, par=par: v.scalar_tensor_tensor(
                          gq[par][:, :].bitcast(BF16), gq[par][:, :], -8.0,
                          gb[par][:, :], ALU.add, ALU.mult),
                      waits=[(s_egq[par], nld), (s_egb[par], nld)],
                      incs=[(s_edq, 1)], fence=True)
                pl.op("vector",
                      lambda v, par=par: v.scalar_tensor_tensor(
                          uq[par][:, :].bitcast(BF16), uq[par][:, :], -8.0,
                          ub[par][:, :], ALU.add, ALU.mult),
                      waits=[(s_euq[par], nld), (s_eub[par], nld)],
                      incs=[(s_edq, 1)], fence=True)

            def e_mms(ic):
                par = ic % 2
                for k in range(4):
                    bki = 4 * par + k
                    src_ = gq[par] if k < 2 else uq[par]
                    ts = k % 2
                    wfree = (s_esi,) if k < 2 else (s_eyt,)

                    def e_mm(t, src=src_, ts=ts, bki=bki):
                        ins = None
                        for j in range(NG):
                            ins = t.matmul(
                                pbank[bki][:, :],
                                src[:, j * 128:(j + 1) * 128].bitcast(BF16),
                                xnt[:, j * TOWN + ts * 512:
                                    j * TOWN + ts * 512 + 512],
                                start=(j == 0), stop=(j == NG - 1))
                        return ins
                    pl.op("tensor", e_mm,
                          waits=[(s_edq, 2 * ic + 1 if k < 2 else 2 * ic + 2),
                                 (wfree[0], 2 * (ic - 2) + k % 2 + 1)],
                          incs=[(s_epe, 1)])
                for ts in range(2):
                    pl.op("scalar",
                          lambda s, par=par, ts=ts: s.activation(
                              sg_sb[par][:, ts * 512:(ts + 1) * 512],
                              pbank[4 * par + ts][:, :], AF.Silu, bias=0.0,
                              scale=1.0),
                          waits=[(s_epe, 4 * ic + ts + 1),
                                 (s_eyt, 2 * (ic - 2) + ts + 1)],
                          incs=[(s_esi, 1)], fence=True)

            def e_yt(ic):
                par = ic % 2
                for ts in range(2):
                    pl.op("vector",
                          lambda v, par=par, ts=ts: v.tensor_mul(
                              yt_sb[par][:, ts * 512:(ts + 1) * 512],
                              sg_sb[par][:, ts * 512:(ts + 1) * 512],
                              pbank[4 * par + 2 + ts][:, :]),
                          waits=[(s_esi, 2 * ic + ts + 1),
                                 (s_epe, 4 * ic + 2 + ts + 1),
                                 (s_dyt[ic % 2], 16 * (ic // 2))],
                          incs=[(s_eyt, 1)], fence=True)
                pl.op("sync",
                      lambda sy, par=par, ic=ic: sy.dma_start(t_ytb[ic],
                                                              yt_sb[par][:, :]),
                      waits=[(s_eyt, s_eyt.v)], incs=[(s_dyt[ic % 2], 16)])
                if "ytb" in taps:
                    pl.op("sync",
                          lambda sy, par=par, ic=ic: sy.dma_start(
                              taps["ytb"][ic], yt_sb[par][:, :]),
                          incs=[(s_tap, 16)])

            e_loads(0)
            e_deq(0)
            for ic in range(NIC):
                if ic + 1 < NIC:
                    e_loads(ic + 1)
                    e_deq(ic + 1)
                e_mms(ic)
                e_yt(ic)
            pl.op("sync", lambda sy: sy.nop(),
                  waits=[(s_dyt[0], s_dyt[0].v), (s_dyt[1], s_dyt[1].v),
                         (s_tap, s_tap.v)])
            pl.run(nc)

        with contextlib.ExitStack() as st4:
            E4 = st4.enter_context
            NBUF = 4
            MACROS = []
            for hs in range(NHS):
                MACROS += [(hs, i * 4, 4) for i in range(21)] + [(hs, 84, 2)]
            NMAC = len(MACROS)
            dqb = [E4(nc.sbuf_tensor(f"dqb{i}", [128, 2048], I16))
                   for i in range(NBUF)]
            db = [E4(nc.sbuf_tensor(f"db{i}", [128, 2048], BF16))
                  for i in range(NBUF)]
            ytl = [E4(nc.sbuf_tensor(f"ytl{i}", [128, 4 * TOWN], BF16))
                   for i in range(NBUF)]
            xev = E4(nc.sbuf_tensor("xev", [128, D], F32))
            aev = E4(nc.sbuf_tensor("aev", [128, D], F32))
            o_sb = [E4(nc.sbuf_tensor(f"osb{i}", [128, 512], F32))
                    for i in range(2)]
            s_fdw = [Sem(st4, nc, f"s_fdw{i}") for i in range(NBUF)]
            s_fdb = [Sem(st4, nc, f"s_fdb{i}") for i in range(NBUF)]
            s_fylp = [Sem(st4, nc, f"s_fylp{i}") for i in range(NBUF)]
            s_dyyp = [Sem(st4, nc, f"s_dyyp{i}") for i in range(2)]
            s_fdq = Sem(st4, nc, "s_fdq")
            s_fpe = Sem(st4, nc, "s_fpe")
            s_fev = Sem(st4, nc, "s_fev")

            def consumed(m):
                if m < 0:
                    return 0
                hs, ic0, nm = MACROS[m]
                return hs * NIC + ic0 + nm

            pl = Plan_()

            def f_loads(m):
                hs, ic0, nm = MACROS[m]
                par = m % NBUF
                free = consumed(m - NBUF)
                pl.op("gpsimd",
                      lambda g, par=par, ic0=ic0, nm=nm, hs=hs:
                      g.dma_start(dqb[par][:, 0:nm * 512],
                                  t_dqt[ic0:ic0 + nm, :,
                                        hs * 512:(hs + 1) * 512]
                                  .rearrange("a p h -> p a h")),
                      waits=[(s_fpe, free)], incs=[(s_fdw[par], 16)])
                pl.op("sync",
                      lambda sy, par=par, ic0=ic0, nm=nm, hs=hs:
                      sy.dma_start(ytl[par][:, 0:nm * TOWN],
                                   t_ytb[ic0:ic0 + nm]
                                   .rearrange("a p t -> p a t")),
                      waits=[(s_fpe, free)], incs=[(s_fylp[par], 16)])
                pl.op("gpsimd",
                      lambda g, par=par, ic0=ic0, nm=nm, hs=hs:
                      g.dma_start(
                          db[par][:, 0:nm * 512],
                          t_dss[hs * 44032 + ic0 * 512:
                                hs * 44032 + (ic0 + nm) * 512]
                          .partition_broadcast(128)),
                      waits=[(s_fdq, m - NBUF + 1)], incs=[(s_fdb[par], 16)])

            def f_deq(m):
                hs, ic0, nm = MACROS[m]
                par = m % NBUF
                nmac = 16 * (m // NBUF + 1)
                pl.op("vector",
                      lambda v, par=par, nm=nm: v.scalar_tensor_tensor(
                          dqb[par][:, 0:nm * 512].bitcast(BF16),
                          dqb[par][:, 0:nm * 512], -8.0,
                          db[par][:, 0:nm * 512], ALU.add, ALU.mult),
                      waits=[(s_fdw[par], nmac), (s_fdb[par], nmac)],
                      incs=[(s_fdq, 1)], fence=True)

            def f_mms(m):
                hs, ic0, nm = MACROS[m]
                par = m % NBUF
                for ici in range(nm):
                    ic = ic0 + ici

                    def f_mm(t, ic=ic, ici=ici, par=par, hs=hs):
                        ins = None
                        for tc in range(8):
                            if ic == 0 and hs > 0:
                                t.wait_ge(s_fev.h, 8 * (hs - 1) + tc + 1)
                            ins = t.matmul(
                                pbank[tc][:, :],
                                ytl[par][:, ici * TOWN + tc * 128:
                                         ici * TOWN + (tc + 1) * 128],
                                dqb[par][:, ici * 512:(ici + 1) * 512]
                                .bitcast(BF16),
                                start=(ic == 0), stop=(ic == NIC - 1))
                        return ins
                    mmw = [(s_fdq, m + 1),
                           (s_fylp[par], 16 * (m // NBUF + 1))]
                    pl.op("tensor", f_mm, waits=mmw, incs=[(s_fpe, 1)])

            def f_xaev(hs):
                pl.op("gpsimd",
                      lambda g, hs=hs: g.dma_start(
                          xev[:, :],
                          t_xs[2:10, :, hs * 512:(hs + 1) * 512]
                          .rearrange("a p h -> p a h")),
                      waits=[(s_fev, 8 * hs)], incs=[(s_dg, 16)])
                pl.op("gpsimd",
                      lambda g, hs=hs: g.dma_start(
                          aev[:, :],
                          t_attnb[0:8, :, hs * 512:(hs + 1) * 512]
                          .rearrange("a p h -> p a h")),
                      incs=[(s_dg, 16)])

            def f_evacs(hs):
                for tc in range(8):
                    gev = hs * 8 + tc

                    def f_ev(v, tc=tc, par2=gev % 2):
                        chain(v, v.tensor_add(o_sb[par2][:, :], pbank[tc][:, :],
                                              xev[:, tc * 512:(tc + 1) * 512]))
                        return v.tensor_add(o_sb[par2][:, :], o_sb[par2][:, :],
                                            aev[:, tc * 512:(tc + 1) * 512])
                    pl.op("vector", f_ev,
                          waits=[(s_fpe, (hs + 1) * NIC), (s_dg, s_dg.v),
                                 (s_dyyp[gev % 2], 16 * (gev // 2))],
                          incs=[(s_fev, 1)], fence=True)
                    pl.op("sync",
                          lambda sy, tc=tc, hs=hs, par2=gev % 2:
                          sy.dma_start(t_y[tc][:, hs * 512:(hs + 1) * 512],
                                       o_sb[par2][:, :]),
                          waits=[(s_fev, s_fev.v)],
                          incs=[(s_dyyp[gev % 2], 16)])

            f_loads(0)
            f_loads(1)
            f_deq(0)
            for m in range(NMAC):
                hs, ic0, nm = MACROS[m]
                if ic0 == 0:
                    f_xaev(hs)
                if m + 2 < NMAC:
                    f_loads(m + 2)
                if m + 1 < NMAC:
                    f_deq(m + 1)
                f_mms(m)
                if ic0 + nm == NIC:
                    f_evacs(hs)
            pl.op("sync", lambda sy: sy.nop(),
                  waits=[(s_dyyp[0], s_dyyp[0].v), (s_dyyp[1], s_dyyp[1].v)])
            pl.run(nc)

    return nc


def _rope_tables():
    inv = (1.0 / (10000.0 ** (np.arange(0, DH, 2, dtype=np.float32) / DH))
           ).astype(np.float32)
    tpos = np.arange(S, dtype=np.float32)
    f = np.outer(tpos, inv)
    cosf = np.zeros((S, DH), np.float32)
    sinf = np.zeros((S, DH), np.float32)
    cosf[:, 0::2] = np.cos(f)
    cosf[:, 1::2] = np.cos(f)
    sinf[:, 0::2] = np.sin(f)
    sinf[:, 1::2] = np.sin(f)
    return cosf, sinf


def _blend_for_core(c):
    bl = np.zeros((128, 20), np.float32)
    r = np.arange(128, dtype=np.float32)
    for w in range(5):
        W = 4 * c - 1 + w
        if W < 0:
            continue
        if w == 0:
            for cc in range(2):
                oq = cc * 128 + r
                bl[:, w * 4 + cc] = 1.0 - oq / 255.0
        elif w == 4:
            for cc in range(2):
                wl = cc * 128 + r
                bl[:, w * 4 + cc] = wl / 255.0
        else:
            for cc in range(4):
                wl = cc * 128 + r
                if cc < 2:
                    val = np.ones(128, np.float32) if W == 0 else wl / 255.0
                else:
                    val = 1.0 - (wl - 256.0) / 255.0
                bl[:, w * 4 + cc] = val
    return bl


def kernel(L0_attn_x, L0_attn_Wq, L0_attn_Wk, L0_attn_Wv, L0_attn_Wo,
           L0_ffn_gate_q, L0_ffn_gate_s, L0_ffn_up_q, L0_ffn_up_s,
           L0_ffn_down_q, L0_ffn_down_s):
    x = np.ascontiguousarray(np.asarray(L0_attn_x).reshape(S, D),
                             dtype=np.float32)
    Wq = np.asarray(L0_attn_Wq, np.float32)
    Wk = np.asarray(L0_attn_Wk, np.float32)
    Wv = np.asarray(L0_attn_Wv, np.float32)
    Wo = np.asarray(L0_attn_Wo, np.float32)
    gate_q = np.asarray(L0_ffn_gate_q, np.int32)
    up_q = np.asarray(L0_ffn_up_q, np.int32)
    down_q = np.asarray(L0_ffn_down_q, np.int32)
    gate_s = np.asarray(L0_ffn_gate_s, np.float32)
    up_s = np.asarray(L0_ffn_up_s, np.float32)
    down_s = np.asarray(L0_ffn_down_s, np.float32)

    wqt = np.ascontiguousarray(
        Wq.reshape(NH, DH, NG, 128).transpose(0, 3, 2, 1).reshape(NH, 128, D))
    wkt = np.ascontiguousarray(
        Wk.reshape(DH, NG, 128).transpose(2, 1, 0).reshape(128, D))
    wvt = np.ascontiguousarray(
        Wv.reshape(DH, NG, 128).transpose(2, 1, 0).reshape(128, D))
    wot = np.ascontiguousarray(
        Wo.reshape(NFS, 256, NG, 128).transpose(0, 3, 2, 1)
        .reshape(NFS, 128, NG * 256))
    gqt = np.ascontiguousarray(
        gate_q.reshape(NG, 128, NIC, 128).transpose(2, 1, 0, 3)
        .reshape(NIC, 128, D))
    uqt = np.ascontiguousarray(
        up_q.reshape(NG, 128, NIC, 128).transpose(2, 1, 0, 3)
        .reshape(NIC, 128, D))
    dqt = np.ascontiguousarray(down_q.reshape(NIC, 128, D))
    gst = np.ascontiguousarray(
        gate_s.reshape(NG, NIC, 128).transpose(1, 0, 2).reshape(128, 2752))
    ust = np.ascontiguousarray(
        up_s.reshape(NG, NIC, 128).transpose(1, 0, 2).reshape(128, 2752))
    dst = np.ascontiguousarray(
        down_s.reshape(NIC, NHS, 512).transpose(1, 0, 2).reshape(128, 2752))
    rmat = np.zeros((128, 128), BF16NP)
    for i in range(128):
        rmat[i, (i + 64) % 128] = 1
    ident = np.eye(128).astype(BF16NP)
    cosf, sinf = _rope_tables()

    in_maps = []
    for c in range(NCORE):
        lo = 1024 * c - 256
        xs = np.zeros((THALO, D), np.float32)
        a, b = max(0, lo), min(S, lo + THALO)
        xs[a - lo:b - lo] = x[a:b]
        cpad = np.zeros((THALO, DH), np.float32)
        spad = np.zeros((THALO, DH), np.float32)
        cpad[a - lo:b - lo] = cosf[a:b]
        spad[a - lo:b - lo] = sinf[a:b]
        cosb = np.ascontiguousarray(cpad.T).astype(BF16NP)
        sinT = np.ascontiguousarray(spad.T)
        sinT[0:64, :] *= -1.0
        sinb = sinT.astype(BF16NP)
        w4m = np.ones((128, 512), BF16NP)
        if c == NCORE - 1:
            w4m[:, 256:] = 0
        in_maps.append(dict(
            xs=xs.reshape(12, 128, D), wqt=wqt, wkt=wkt, wvt=wvt, wot=wot,
            gqt=gqt, uqt=uqt, dqt=dqt, gst=gst, ust=ust, dst=dst,
            cosb=cosb, sinb=sinb, blend=_blend_for_core(c), w4m=w4m,
            rmat=rmat, ident=ident))

    nc = build_nc()
    res = run_bass_kernel_spmd(nc, in_maps, core_ids=list(range(NCORE)))
    y = np.concatenate(
        [np.asarray(r["y"], np.float32).reshape(TOWN, D)
         for r in res.results], axis=0)
    if DEBUG_TAPS:
        kernel.last_results = res
    return y.reshape(1, S, D)
